# version 13
# baseline (speedup 1.0000x reference)
import numpy as np
from contextlib import ExitStack

import concourse.bass as bass
import concourse.mybir as mybir
from concourse.bass_utils import run_bass_kernel_spmd

F32 = mybir.dt.float32
F32R = mybir.dt.float32r
BF16 = mybir.dt.bfloat16
FP16 = mybir.dt.float16
I32 = mybir.dt.int32
AF = mybir.ActivationFunctionType
ALU = mybir.AluOpType
AX = mybir.AxisListType

B_FULL, L, D = 16, 2048, 1024
NCORES = 8
B_CORE = B_FULL // NCORES
NST = 16
RK = 64
INNER = 4096
ND = 8
STG_T = 512
N_STAGES = (B_CORE * L) // STG_T
TT = STG_T // 128
EPS = 1e-5

ENG_SEM = {"tensor": "pe", "vector": "dve", "scalar": "act", "gpsimd": "pool"}


class Buf:
    __slots__ = ("name", "wtok", "rtoks")

    def __init__(self, name):
        self.name = name
        self.wtok = None
        self.rtoks = []


class Plan:
    def __init__(self):
        self.ops = []
        self.cnt = {}
        self.wm = {}

    def op(self, eng, fn, reads=(), writes=(), sem=None, k=1, deps=(), sync=False):
        if sem is None:
            sem = ENG_SEM[eng]
            k = 1
        waits = {}

        def need(tok):
            if tok is None:
                return
            s, v = tok
            if s == ENG_SEM.get(eng) and not sync:
                return
            if self.wm.get((eng, s), 0) >= v:
                return
            waits[s] = max(waits.get(s, 0), v)

        for b in reads:
            need(b.wtok)
        for b in writes:
            need(b.wtok)
            for t in b.rtoks:
                need(t)
        for t in deps:
            need(t)
        wlist = list(waits.items())
        for s, v in wlist:
            self.wm[(eng, s)] = v
        self.cnt[sem] = self.cnt.get(sem, 0) + k
        tok = (sem, self.cnt[sem])
        self.ops.append((eng, wlist, fn, (sem, k)))
        for b in reads:
            b.rtoks.append(tok)
        for b in writes:
            b.wtok = tok
            b.rtoks = []
        return tok

    def raw_wait(self, eng, toks):
        wlist = []
        for s, v in toks:
            if self.wm.get((eng, s), 0) < v:
                wlist.append((s, v))
                self.wm[(eng, s)] = v
        if wlist:
            self.ops.append((eng, wlist, None, None))


def build_program(n_stages=N_STAGES, do_prologue=True, do_mixer=True, do_ffn=True, mixer_upto=99, xp_dbs=ND, xp_tail=2, tap=None, fences=7, overlap=True, use_park=True, qpsum=True):
    nc = bass.Bass("TRN2", target_bir_lowering=False, debug=False)

    d_x = nc.dram_tensor("x", [B_CORE, L, D], F32, kind="ExternalInput").ap()
    d_wx = nc.dram_tensor("w_x", [D, 96], F32, kind="ExternalInput").ap()
    d_wdt = nc.dram_tensor("w_dt", [RK, D], F32, kind="ExternalInput").ap()
    d_bdt = nc.dram_tensor("b_dt_fm", [128, ND], F32, kind="ExternalInput").ap()
    d_alog = nc.dram_tensor("a_log_fm", [128, ND * NST], F32, kind="ExternalInput").ap()
    d_dfm = nc.dram_tensor("d_fm", [128, ND], F32, kind="ExternalInput").ap()
    d_wg = nc.dram_tensor("w_gate", [INNER, D], F32, kind="ExternalInput").ap()
    d_wu = nc.dram_tensor("w_up", [INNER, D], F32, kind="ExternalInput").ap()
    d_wd = nc.dram_tensor("w_down", [D, INNER], F32, kind="ExternalInput").ap()
    d_id = nc.dram_tensor("ident", [128, 128], F32, kind="ExternalInput").ap()
    d_y = nc.dram_tensor("y", [B_CORE, L, D], F32, kind="ExternalOutput").ap()
    s_wgT = nc.dram_tensor("scr_wgT", [D, INNER], BF16, kind="Internal").ap()
    s_wuT = nc.dram_tensor("scr_wuT", [D, INNER], BF16, kind="Internal").ap()
    s_wdT = nc.dram_tensor("scr_wdT", [INNER, D], BF16, kind="Internal").ap()
    s_bc = nc.dram_tensor("scr_bc", [2 * NST, STG_T], FP16, kind="Internal").ap()
    s_pk = [nc.dram_tensor(f"scr_park{i}", [STG_T, D], F32, kind="Internal").ap()
            for i in range(2)]

    P = Plan()

    with ExitStack() as stack:
        def sb(name, shape, dtype=F32):
            return stack.enter_context(nc.sbuf_tensor(name, shape, dtype))

        wx_r = sb("wx_r", [128, ND * 96], F32R)
        wdt_r = sb("wdt_r", [RK, D], F32R)
        a_neg = sb("a_neg", [128, ND * NST], F32)
        bdt_fm = sb("bdt_fm", [128, ND], F32)
        dp1_fm = sb("dp1_fm", [128, ND], F32)
        ident = sb("identsb", [128, 128], F32)
        identb = sb("identb", [128, 128], BF16)
        ones_col = sb("ones_col", [128, 1], F32)
        ones_row = sb("ones_row", [1, 128], F32)
        zero_col = sb("zero_col", [128, 1], F32)
        escr = sb("escr", [128, 2], F32)
        ascr = sb("ascr", [128, 2], F32)
        hstA = sb("hstA", [128, 128], F32)
        hstB = sb("hstB", [128, 128], F32)
        mw127 = sb("mw127", [128, 3], F32)
        swcol = sb("swcol", [128, 3], F32)
        x_stage = sb("x_stage", [128, TT * D], F32)
        x_tmp = sb("x_tmp", [128, 2 * STG_T], F32R)
        dtx_db = sb("dtx_db", [128, STG_T], F32)
        dt_db = sb("dt_db", [128, STG_T], F32)
        yac_db = sb("yac_db", [128, STG_T], F32)
        h_grp = sb("h_grp", [128, 2 * STG_T], F32)
        ln1 = sb("ln1", [128, TT * D], F32)
        mxs = sb("mxs", [128, 4 * STG_T], F32)
        brep = sb("brep", [128, NST * STG_T], FP16)
        crep = sb("crep", [128, NST * STG_T], FP16)
        xp_f = sb("xp_f", [96, STG_T], F32)
        dtr_r = sb("dtr_r", [128, STG_T], F32R)
        xqT = sb("xqT", [128, TT * D], BF16)
        Pbuf = sb("Pbuf", [128, TT * INNER], F32)
        qPT = sb("qPT", [128, INNER], BF16)
        gsb = sb("gsb", [128, 256], F32)
        wg_sb = sb("wg_sb", [128, 2 * 8 * 256], BF16)
        wu_sb = sb("wu_sb", [128, 2 * 8 * 256], BF16)
        wd_sb = sb("wd_sb", [128, 2 * D], BF16)
        out_t = sb("out_t", [128, D], F32)
        cols = sb("cols", [128, 24], F32)
        bnst = sb("bnst", [128, 12], F32)
        mv = sb("mv", [128, 4], F32)
        mv2 = sb("mv2", [128, 4], F32)
        bnst2 = sb("bnst2", [128, 12], F32)
        pool_scr = sb("pool_scr", [128, 4], F32)
        s1x1 = sb("s1x1", [1, 4], F32)
        epsc = sb("epsc", [128, 1], F32)

        pb = [stack.enter_context(nc.psum_tensor(f"pb{i}", [128, 512], F32))
              for i in range(8)]

        sems = {}
        for s in ["pe", "dve", "act", "pool", "dqs", "dqp", "dout", "dx", "dbr", "dwA", "dwd", "dsc", "dwt"]:
            sems[s] = stack.enter_context(nc.semaphore("sem_" + s))

        block = stack.enter_context(nc.Block())

        bufs = {}

        def B_(name):
            if name not in bufs:
                bufs[name] = Buf(name)
            return bufs[name]

        pbB = [B_(f"pb{i}") for i in range(8)]

        def dma(queue, out, in_, reads=(), writes=(), sem=None, deps=()):
            eng = {"s": "sync", "p": "gpsimd"}[queue]
            if sem is None:
                sem = {"s": "dqs", "p": "dqp"}[queue]
            return P.op(eng, lambda e, o=out, i=in_: e.dma_start(out=o, in_=i),
                        reads=reads, writes=writes, sem=sem, k=16, deps=deps)

        def act(fn, reads=(), writes=(), deps=(), sync=False):
            return P.op("scalar", fn, reads=reads, writes=writes, deps=deps, sync=sync)

        def dve(fn, reads=(), writes=(), deps=(), sync=False):
            return P.op("vector", fn, reads=reads, writes=writes, deps=deps, sync=sync)

        def pool(fn, reads=(), writes=(), deps=(), sync=False):
            return P.op("gpsimd", fn, reads=reads, writes=writes, deps=deps, sync=sync)

        def pe(fn, reads=(), writes=(), deps=(), sync=False):
            return P.op("tensor", fn, reads=reads, writes=writes, deps=deps, sync=sync)

        def dve_echo():
            return dve(lambda e: e.tensor_copy(out=escr[:], in_=zero_col[:].broadcast_to([128, 2])),
                       reads=[B_("zero_col")], writes=[B_("escr")])

        def act_echo():
            return act(lambda e: e.activation(out=ascr[:, 0:1], in_=zero_col[:],
                                              func=AF.Identity),
                       reads=[B_("zero_col")], writes=[B_("ascr")])

        colsB = B_("cols")
        bnB = B_("bnst")
        bn2B = B_("bnst2")
        mv2B = B_("mv2")
        mvB = B_("mv")

        dma("s", a_neg[:], d_alog, writes=[B_("a_neg")])
        dma("s", bdt_fm[:], d_bdt, writes=[B_("bdt_fm")])
        dma("s", dp1_fm[:], d_dfm, writes=[B_("dp1_fm")])
        dma("s", ident[:], d_id, writes=[B_("ident")])
        for db in range(ND):
            dma("s", x_stage[:, db * 96:(db + 1) * 96],
                d_wx[db * 128:(db + 1) * 128, :], writes=[B_("xs_pro")])
        dma("s", x_stage[0:RK, 1024:2048], d_wdt, writes=[B_("xs_pro")])
        _par_total = ("dqs", P.cnt.get("dqs", 0))
        for _nm in ["a_neg", "bdt_fm", "dp1_fm", "ident", "xs_pro"]:
            B_(_nm).wtok = _par_total
        dve(lambda e: e.tensor_copy(out=wx_r[:], in_=x_stage[:, 0:ND * 96]),
            reads=[B_("xs_pro")], writes=[B_("wx_r")])
        dve(lambda e: e.tensor_copy(out=wdt_r[:], in_=x_stage[0:RK, 1024:2048]),
            reads=[B_("xs_pro")], writes=[B_("wdt_r")])
        act(lambda e: e.activation(out=a_neg[:], in_=a_neg[:], func=AF.Exp),
            reads=[B_("a_neg")], writes=[B_("a_neg")])
        dve(lambda e: e.tensor_scalar(out=a_neg[:], in0=a_neg[:], scalar1=-1.0,
                                      scalar2=None, op0=ALU.mult),
            reads=[B_("a_neg")], writes=[B_("a_neg")])
        dve(lambda e: e.tensor_scalar(out=dp1_fm[:], in0=dp1_fm[:], scalar1=1.0,
                                      scalar2=None, op0=ALU.add),
            reads=[B_("dp1_fm")], writes=[B_("dp1_fm")])
        dve(lambda e: e.tensor_copy(out=identb[:], in_=ident[:]),
            reads=[B_("ident")], writes=[B_("identb")])
        dve(lambda e: e.memset(ones_col[:], 1.0), writes=[B_("ones_col")])
        dve(lambda e: e.memset(ones_row[:], 1.0), writes=[B_("ones_row")])
        dve(lambda e: e.memset(zero_col[:], 0.0), writes=[B_("zero_col")])
        dve(lambda e: e.memset(epsc[:], EPS), writes=[B_("epsc")])

        pbufB = B_("Pbuf_pro")
        wtile = [Pbuf[:, i * 512:(i + 1) * 512] for i in range(2)]
        wqi = Pbuf[:, 1024:1536].bitcast(I32)
        wqb = Pbuf[:, 1536:1792].bitcast(BF16)
        wacc = [Pbuf[:, 1792 + i:1793 + i] for i in range(2)]
        wqT_ev = [Pbuf[:, 2048 + i * 64:2048 + (i + 1) * 64].bitcast(BF16)
                  for i in range(2)]

        for (dw, scr, NR, NC_, wi) in ([(d_wg, s_wgT, INNER, D, 0),
                                       (d_wu, s_wuT, INNER, D, 1),
                                       (d_wd, s_wdT, D, INNER, 2)] if do_prologue else []):
            ntile = (NR // 128) * (NC_ // 512)
            ncb = NC_ // 512
            dve(lambda e: e.memset(Pbuf[:, 1792:1794], 0.0), writes=[pbufB])
            for j in range(ntile):
                r0, c0 = (j // ncb) * 128, (j % ncb) * 512
                tb = j % 2
                dma("s", wtile[tb], dw[r0:r0 + 128, c0:c0 + 512], writes=[pbufB], sem="dwt")
                dve(lambda e, tb=tb: e.tensor_reduce(
                        out=cols[:, 8 + tb:9 + tb], in_=wtile[tb], axis=AX.X,
                        op=ALU.add, apply_absolute_value=True),
                    reads=[pbufB], writes=[colsB])
                dve(lambda e, tb=tb: e.tensor_tensor(
                        out=wacc[tb], in0=wacc[tb], in1=cols[:, 8 + tb:9 + tb],
                        op=ALU.add),
                    reads=[colsB, pbufB], writes=[pbufB], sync=True)
            dve(lambda e: e.tensor_tensor(out=wacc[0], in0=wacc[0], in1=wacc[1],
                                          op=ALU.add),
                reads=[pbufB], writes=[pbufB], sync=True)
            pe(lambda e: e.matmul(out=pb[0][0:1, 0:1], lhsT=wacc[0], rhs=ones_col[:],
                                  start=True, stop=True),
               reads=[pbufB, B_("ones_col")], writes=[pbB[0]])
            act(lambda e, NR=NR, NC_=NC_, wi=wi: e.activation(
                    out=s1x1[:, wi:wi + 1], in_=pb[0][0:1, 0:1],
                    func=AF.Identity, scale=1.0 / float(NR * NC_)),
                reads=[], writes=[pbB[0], B_("s1x1")])
            dve(lambda e, wi=wi: e.tensor_scalar(
                    out=s1x1[:, wi:wi + 1], in0=s1x1[:, wi:wi + 1],
                    scalar1=1e-5, scalar2=None, op0=ALU.max),
                reads=[B_("s1x1")], writes=[B_("s1x1")])
            pe(lambda e, wi=wi: e.matmul(out=pb[0][:, 0:1], lhsT=ones_row[:],
                                         rhs=s1x1[0:1, wi:wi + 1],
                                         start=True, stop=True),
               reads=[B_("s1x1"), B_("ones_row")], writes=[pbB[0]])
            act(lambda e, wi=wi: e.activation(out=mw127[:, wi:wi + 1],
                                              in_=pb[0][:, 0:1],
                                              func=AF.Identity, scale=1.0 / 127.0),
                reads=[], writes=[pbB[0], B_("mw127")])
            act(lambda e: e.activation(out=cols[:, 15:16], in_=pb[0][:, 0:1],
                                       func=AF.Identity),
                reads=[], writes=[pbB[0], colsB])
            dve(lambda e, wi=wi: e.reciprocal(out=swcol[:, wi:wi + 1],
                                              in_=cols[:, 15:16]),
                reads=[colsB], writes=[B_("swcol")])
            for j in range(ntile):
                r0, c0 = (j // ncb) * 128, (j % ncb) * 512
                tb = j % 2
                dma("s", wtile[tb], dw[r0:r0 + 128, c0:c0 + 512], writes=[pbufB], sem="dwt")
                act(lambda e, tb=tb, wi=wi: e.activation(
                        out=wqi, in_=wtile[tb], func=AF.Identity,
                        scale=swcol[:, wi:wi + 1]),
                    reads=[pbufB, B_("swcol")], writes=[pbufB])
                dve(lambda e: e.tensor_scalar(out=wqb, in0=wqi, scalar1=1.0,
                                              scalar2=-1.0, op0=ALU.min,
                                              op1=ALU.max),
                    reads=[pbufB], writes=[pbufB])
                ec_wq = dve_echo()
                for cb in range(4):
                    pk = 2 + (cb % 2)
                    pe(lambda e, cb=cb, pk=pk: e.transpose(
                            out=pb[pk][:, 0:64].bitcast(BF16),
                            in_=wqb[:, cb * 128:(cb + 1) * 128],
                            identity=identb[:]),
                       reads=[pbufB, B_("identb")], writes=[pbB[pk]], deps=[ec_wq])
                    act(lambda e, cb=cb, pk=pk: e.activation(
                            out=wqT_ev[cb % 2], in_=pb[pk][:, 0:64].bitcast(BF16),
                            func=AF.Identity),
                        reads=[], writes=[pbB[pk], pbufB])
                    dma("s", scr[c0 + cb * 128:c0 + (cb + 1) * 128, r0:r0 + 128],
                        wqT_ev[cb % 2], reads=[pbufB], sem="dsc")

        if tap == "wgq":
            for blkr in range(8):
                t_ld = dma("s", Pbuf[:, 0:512].bitcast(BF16),
                           s_wgT[blkr * 128:(blkr + 1) * 128, 0:1024],
                           writes=[B_("Pbuf_pro")])
                t_cv = P.op("vector",
                            lambda e: e.tensor_copy(out=Pbuf[:, 4096:5120],
                                                    in_=Pbuf[:, 0:512].bitcast(BF16)),
                            reads=[B_("Pbuf_pro")], writes=[B_("Pbuf_pro")], sync=True)
                dma("s", d_y[0, blkr * 128:(blkr + 1) * 128, :], Pbuf[:, 4096:5120],
                    reads=[B_("Pbuf_pro")], sem="dout")
            P.raw_wait("gpsimd", [("dout", 16 * 8)])
        xstgB = [B_(f"xstg{t}") for t in range(TT)]
        xtmpB = [B_("x_tmp0"), B_("x_tmp1")]
        dtxB = B_("dtx_db")
        dtB = B_("dt_db")
        yacB = B_("yac_db")
        hgB = B_("h_grp")
        ln1B = [B_(f"ln1_{t}") for t in range(TT)]
        mxsB = [B_(f"mxs{i}") for i in range(4)]
        brepB = B_("brep")
        crepB = B_("crep")
        xpB = B_("xp_f")
        dtrB = B_("dtr_r")
        hstB_ = [B_("hstA"), B_("hstB")]
        xqTB = [B_(f"xqT{t}") for t in range(TT)]
        PB = [B_(f"P{t}") for t in range(TT)]
        qPTB = [B_("qPT0")]
        gsB = [B_("gs0"), B_("gs0")]
        wgB = [B_("wg0"), B_("wg1")]
        wuB = [B_("wu0"), B_("wu1")]
        wdB = [B_("wd0"), B_("wd1")]
        outB = [B_("out0")]
        bcB = B_("s_bc")

        for t in range(TT):
            if pbufB.wtok is not None:
                PB[t].rtoks.append(pbufB.wtok)
            PB[t].rtoks.extend(pbufB.rtoks)
        xspro = B_("xs_pro")
        for t in range(TT):
            if xspro.wtok is not None:
                xstgB[t].rtoks.append(xspro.wtok)
            xstgB[t].rtoks.extend(xspro.rtoks)

        mxs_dA = [mxs[:, 0:512], mxs[:, 512:1024]]
        mxs_dBx = [mxs[:, 1024:1536], mxs[:, 1536:2048]]
        i32s = mxs[:, 0:512].bitcast(I32)
        x_q = mxs[:, 1024:1536].bitcast(BF16)

        hst = [hstA, hstB]
        xqT_echo = {}
        qPT_echo = {}

        def pe_zfence(bank):
            return pe(lambda e, bank=bank: e.matmul(
                    out=pb[bank][0:1, 0:1], lhsT=zero_col[0:1, :],
                    rhs=ones_col[0:1, :], start=False, stop=True),
                reads=[B_("zero_col"), B_("ones_col")], writes=[pbB[bank]])

        def plan_mv_combine(scr, mvx, bnx, mvxB, bnxB):
            dve(lambda e: e.bn_aggr(out=mvx[:, 0:2], in_=bnx[:, 0:6]),
                reads=[bnxB], writes=[mvxB], sync=True)
            dve(lambda e: e.bn_aggr(out=mvx[:, 2:4], in_=bnx[:, 6:12]),
                reads=[bnxB], writes=[mvxB], sync=True)
            dve(lambda e: e.tensor_tensor(out=cols[:, scr:scr + 1], in0=mvx[:, 0:1],
                                          in1=mvx[:, 2:3], op=ALU.subtract),
                reads=[mvxB], writes=[colsB], sync=True)
            dve(lambda e: e.tensor_tensor(out=cols[:, scr:scr + 1], in0=cols[:, scr:scr + 1],
                                          in1=cols[:, scr:scr + 1], op=ALU.mult),
                reads=[colsB], writes=[colsB], sync=True)
            dve(lambda e: e.tensor_tensor(out=mvx[:, 0:1], in0=mvx[:, 0:1],
                                          in1=mvx[:, 2:3], op=ALU.add),
                reads=[mvxB], writes=[mvxB], sync=True)
            dve(lambda e: e.tensor_scalar(out=mvx[:, 0:1], in0=mvx[:, 0:1],
                                          scalar1=0.5, scalar2=None, op0=ALU.mult),
                reads=[mvxB], writes=[mvxB], sync=True)
            dve(lambda e: e.tensor_tensor(out=mvx[:, 1:2], in0=mvx[:, 1:2],
                                          in1=mvx[:, 3:4], op=ALU.add),
                reads=[mvxB], writes=[mvxB], sync=True)
            dve(lambda e: e.scalar_tensor_tensor(out=mvx[:, 1:2], in0=cols[:, scr:scr + 1],
                                                 scalar=0.5, in1=mvx[:, 1:2],
                                                 op0=ALU.mult, op1=ALU.add),
                reads=[colsB, mvxB], writes=[mvxB], sync=True)
            dve(lambda e: e.tensor_scalar(out=mvx[:, 1:2], in0=mvx[:, 1:2],
                                          scalar1=0.5, scalar2=None, op0=ALU.mult),
                reads=[mvxB], writes=[mvxB], sync=True)

        def plan_mixer(st, upto=99, tap=tap):
            b = st // (L // STG_T)
            t0 = (st % (L // STG_T)) * STG_T
            seq_start = (st % (L // STG_T)) == 0
            hc, hp = hst[st % 2], hst[1 - st % 2]
            hcB, hpB = hstB_[st % 2], hstB_[1 - st % 2]

            for tt in range(TT):
                dma("s", x_stage[:, tt * D:(tt + 1) * D],
                    d_x[b, t0 + tt * 128: t0 + (tt + 1) * 128, :],
                    writes=[xstgB[tt]], sem="dx")

            for db in range(ND):
                pk = db % 2
                for tt in range(TT):
                    pe(lambda e, db=db, tt=tt, pk=pk: e.transpose(
                            out=pb[pk][:, tt * 128:(tt + 1) * 128],
                            in_=x_stage[:, tt * D + db * 128: tt * D + (db + 1) * 128],
                            identity=ident[:]),
                       reads=[xstgB[tt], B_("ident")], writes=[pbB[pk]])
                xv = x_tmp[:, (db % 2) * STG_T:(db % 2 + 1) * STG_T]
                dve(lambda e, pk=pk, xv=xv: e.tensor_copy(out=xv, in_=pb[pk][:]),
                    reads=[], writes=[pbB[pk], xtmpB[db % 2]])
                ec = dve_echo()
                pe(lambda e, db=db, xv=xv: e.matmul(
                        out=pb[2][0:96, :], lhsT=wx_r[:, db * 96:(db + 1) * 96],
                        rhs=xv, start=(db == 0), stop=(db == ND - 1)),
                   reads=[xtmpB[db % 2], B_("wx_r")], writes=[pbB[2]], deps=[ec])
            pe(lambda e: e.matmul(out=pb[1][0:1, 0:1], lhsT=ones_col[0:1, :],
                                  rhs=ones_col[0:1, :], start=True, stop=True),
               reads=[B_("ones_col")], writes=[pbB[1], pbB[2]])
            act(lambda e: e.activation(out=xp_f[:], in_=pb[2][0:96, :],
                                       func=AF.Identity),
                reads=[], writes=[pbB[2], xpB])
            dve(lambda e: e.tensor_copy(out=dtr_r[:], in_=pb[2][:]),
                reads=[], writes=[pbB[2], dtrB])
            ec_dtr = dve_echo()
            yield

            dma("p", s_bc, xp_f[64:96, :], reads=[xpB], writes=[bcB])
            for n in range(NST):
                dma("s", brep[:, n * STG_T:(n + 1) * STG_T],
                    s_bc[n:n + 1, :].broadcast_to([128, STG_T]),
                    reads=[bcB], writes=[brepB], sem="dbr")
                dma("s", crep[:, n * STG_T:(n + 1) * STG_T],
                    s_bc[NST + n:NST + n + 1, :].broadcast_to([128, STG_T]),
                    reads=[bcB], writes=[crepB], sem="dbr")
            yield

            for db in range(ND):
                pk = db % 2
                for tt in range(TT):
                    pe(lambda e, db=db, tt=tt, pk=pk: e.transpose(
                            out=pb[pk][:, tt * 128:(tt + 1) * 128],
                            in_=x_stage[:, tt * D + db * 128: tt * D + (db + 1) * 128],
                            identity=ident[:]),
                       reads=[xstgB[tt], B_("ident")], writes=[pbB[pk]])
                xv = x_tmp[:, (db % 2) * STG_T:(db % 2 + 1) * STG_T]
                dve(lambda e, pk=pk, xv=xv: e.tensor_copy(out=xv, in_=pb[pk][:]),
                    reads=[], writes=[pbB[pk], xtmpB[db % 2]])
                pk2 = 2 + db % 2
                pe(lambda e, db=db, pk2=pk2: e.matmul(
                        out=pb[pk2][:], lhsT=wdt_r[:, db * 128:(db + 1) * 128],
                        rhs=dtr_r[0:RK, :], start=True, stop=True),
                   reads=[dtrB, B_("wdt_r")], writes=[pbB[pk2]], deps=[ec_dtr])
                pe(lambda e, pk=pk: e.matmul(out=pb[pk][0:1, 0:1], lhsT=ones_col[0:1, :],
                                             rhs=ones_col[0:1, :], start=True, stop=True),
                   reads=[B_("ones_col")], writes=[pbB[pk], pbB[pk2]])
                act(lambda e, db=db, pk2=pk2: e.activation(
                        out=dt_db[:], in_=pb[pk2][:], func=AF.Exp,
                        bias=bdt_fm[:, db:db + 1]),
                    reads=[B_("bdt_fm")], writes=[pbB[pk2], dtB])
                act(lambda e: e.activation(out=dt_db[:], in_=dt_db[:], func=AF.Ln,
                                           bias=1.0),
                    reads=[dtB], writes=[dtB])
                dve(lambda e, xv=xv: e.tensor_tensor(out=dtx_db[:], in0=dt_db[:],
                                               in1=xv.bitcast(F32),
                                               op=ALU.mult),
                     reads=[dtB, xtmpB[db % 2]], writes=[dtxB])

                def emit_dA(n):
                    ab = n % 2
                    col = db * NST + n
                    act(lambda e, ab=ab, col=col: e.activation(
                            out=mxs_dA[ab], in_=dt_db[:], func=AF.Exp,
                            scale=a_neg[:, col:col + 1]),
                        reads=[dtB, B_("a_neg")], writes=[mxsB[ab]])

                def emit_dBx(n):
                    ab = n % 2
                    return dve(lambda e, n=n, ab=ab: e.tensor_tensor(
                            out=mxs_dBx[ab], in0=dtx_db[:],
                            in1=brep[:, n * STG_T:(n + 1) * STG_T], op=ALU.mult),
                         reads=[dtxB, brepB], writes=[mxsB[2 + ab]])

                emit_dA(0)
                fence = emit_dBx(0)
                for n in range(NST):
                    ab = n % 2
                    if n + 1 < NST:
                        emit_dA(n + 1)
                        fence = emit_dBx(n + 1)
                    else:
                        fence = pool(lambda e: e.tensor_copy(out=pool_scr[:],
                                                             in_=dtx_db[:, 0:4]),
                                     reads=[dtxB], writes=[B_("pool_scr")])
                    hsl = h_grp[:, (n % 2) * 512:(n % 2 + 1) * 512]
                    col = db * NST + n
                    if seq_start:
                        dve(lambda e, hsl=hsl, ab=ab: e.tensor_tensor_scan(
                                out=hsl, data0=mxs_dA[ab], data1=mxs_dBx[ab],
                                initial=0.0, op0=ALU.mult, op1=ALU.add),
                            reads=[mxsB[ab], mxsB[2 + ab]], writes=[hgB],
                            deps=[fence])
                    else:
                        dve(lambda e, hsl=hsl, ab=ab, col=col, hp=hp:
                                e.tensor_tensor_scan(
                                out=hsl, data0=mxs_dA[ab], data1=mxs_dBx[ab],
                                initial=hp[:, col:col + 1],
                                op0=ALU.mult, op1=ALU.add),
                            reads=[mxsB[ab], mxsB[2 + ab], hpB], writes=[hgB],
                            deps=[fence])
                    dve(lambda e, hsl=hsl, ab=ab, n=n: e.tensor_tensor(
                            out=mxs_dA[ab], in0=hsl,
                            in1=crep[:, n * STG_T:(n + 1) * STG_T], op=ALU.mult),
                        reads=[hgB, crepB], writes=[mxsB[ab]])
                    if n == 0:
                        dve(lambda e, ab=ab: e.tensor_copy(out=yac_db[:],
                                                           in_=mxs_dA[ab]),
                            reads=[mxsB[ab]], writes=[yacB])
                    else:
                        dve(lambda e, ab=ab: e.tensor_tensor(
                                out=yac_db[:], in0=yac_db[:], in1=mxs_dA[ab],
                                op=ALU.add),
                            reads=[mxsB[ab], yacB], writes=[yacB])
                    if n % 2 == 1:
                        g = n // 2
                        dve(lambda e, g=g, db=db, hc=hc: e.tensor_copy(
                                out=hc[:, db * NST + g * 2: db * NST + (g + 1) * 2],
                                in_=h_grp[:, 511::512]),
                            reads=[hgB], writes=[hcB])
                dve(lambda e, db=db, xv=xv: e.scalar_tensor_tensor(
                        out=yac_db[:], in0=xv.bitcast(F32),
                        scalar=dp1_fm[:, db:db + 1], in1=yac_db[:],
                        op0=ALU.mult, op1=ALU.add),
                    reads=[xtmpB[db % 2], B_("dp1_fm"), yacB], writes=[yacB])
                ec_y = dve_echo()
                for tt in range(TT):
                    pe(lambda e, tt=tt, pk=pk: e.transpose(
                            out=pb[pk][:, tt * 128:(tt + 1) * 128],
                            in_=yac_db[:, tt * 128:(tt + 1) * 128],
                            identity=ident[:]),
                       reads=[yacB, B_("ident")], writes=[pbB[pk]], deps=[ec_y])
                for tt in range(TT):
                    dve(lambda e, tt=tt, db=db, pk=pk: e.tensor_copy(
                            out=ln1[:, tt * D + db * 128: tt * D + (db + 1) * 128],
                            in_=pb[pk][:, tt * 128:(tt + 1) * 128]),
                        reads=[], writes=[pbB[pk], ln1B[tt]])
                yield

            if tap == "mixres":
                for tt in range(TT):
                    dma("s", d_y[b, t0 + tt * 128: t0 + (tt + 1) * 128, :],
                        ln1[:, tt * D:(tt + 1) * D], reads=[ln1B[tt]], sem="dout")
                return
            if upto < 6:
                return

            for tt in range(TT):
                lnt = ln1[:, tt * D:(tt + 1) * D]
                dve(lambda e, lnt=lnt: e.bn_stats(out=bnst[:, 0:6],
                                                  in_=lnt[:, 0:512]),
                    reads=[ln1B[tt]], writes=[bnB])
                dve(lambda e, lnt=lnt: e.bn_stats(out=bnst[:, 6:12],
                                                  in_=lnt[:, 512:1024]),
                    reads=[ln1B[tt]], writes=[bnB])
                plan_mv_combine(15, mv, bnst, mvB, bnB)
                act(lambda e: e.activation(out=cols[:, 0:1], in_=mv[:, 1:2],
                                           func=AF.Ln, bias=epsc[:]),
                    reads=[mvB, B_("epsc")], writes=[colsB])
                act(lambda e: e.activation(out=cols[:, 0:1], in_=cols[:, 0:1],
                                           func=AF.Exp, scale=-0.5),
                    reads=[colsB], writes=[colsB], sync=True)
                dve(lambda e: e.scalar_tensor_tensor(out=cols[:, 1:2],
                                                     in0=mv[:, 0:1], scalar=-1.0,
                                                     in1=cols[:, 0:1],
                                                     op0=ALU.mult, op1=ALU.mult),
                    reads=[mvB, colsB], writes=[colsB], sync=True)
                act(lambda e, lnt=lnt: e.activation(out=lnt, in_=lnt,
                                                    func=AF.Identity,
                                                    scale=cols[:, 0:1],
                                                    bias=cols[:, 1:2]),
                    reads=[colsB, ln1B[tt]], writes=[ln1B[tt]], sync=True)
                if tap == "ln1":
                    dma("s", d_y[b, t0 + tt * 128: t0 + (tt + 1) * 128, :],
                        lnt, reads=[ln1B[tt]], sem="dout")
                dve(lambda e, lnt=lnt: e.reduce_max(out=cols[:, 2:3], in_=lnt,
                                                    axis=AX.X,
                                                    apply_absolute_value=True),
                    reads=[ln1B[tt]], writes=[colsB])
                dve(lambda e: e.tensor_scalar(out=cols[:, 2:3], in0=cols[:, 2:3],
                                              scalar1=1e-5, scalar2=None,
                                              op0=ALU.max),
                    reads=[colsB], writes=[colsB], sync=True)
                dve(lambda e: e.reciprocal(out=cols[:, 3:4], in_=cols[:, 2:3]),
                    reads=[colsB], writes=[colsB], sync=True)
                dve(lambda e: e.tensor_scalar(out=cols[:, 3:4], in0=cols[:, 3:4],
                                              scalar1=127.0, scalar2=None,
                                              op0=ALU.mult),
                    reads=[colsB], writes=[colsB], sync=True)
                sg0 = (4 if st % 2 == 0 else 16) + tt
                su0 = (8 if st % 2 == 0 else 20) + tt
                dve(lambda e, sg0=sg0: e.tensor_tensor(out=cols[:, sg0:sg0 + 1],
                                                     in0=cols[:, 2:3],
                                                     in1=mw127[:, 0:1],
                                                     op=ALU.mult),
                    reads=[colsB, B_("mw127")], writes=[colsB], sync=True)
                dve(lambda e, su0=su0: e.tensor_tensor(out=cols[:, su0:su0 + 1],
                                                     in0=cols[:, 2:3],
                                                     in1=mw127[:, 1:2],
                                                     op=ALU.mult),
                    reads=[colsB, B_("mw127")], writes=[colsB], sync=True)
                for h2 in range(2):
                    sl = lnt[:, h2 * 512:(h2 + 1) * 512]
                    act(lambda e, sl=sl: e.activation(out=i32s, in_=sl,
                                                      func=AF.Identity,
                                                      scale=cols[:, 3:4]),
                        reads=[ln1B[tt], colsB], writes=[mxsB[0]])
                    act(lambda e, h2=h2: e.activation(
                            out=x_q[:, h2 * 512:(h2 + 1) * 512], in_=i32s,
                            func=AF.Identity),
                        reads=[mxsB[0]], writes=[mxsB[2]])
                ec_xq = act_echo()
                xtoks = []

                def xq_transpose(hb):
                    pkq = hb % 2
                    t = pe(lambda e, hb=hb, pkq=pkq: e.transpose(
                            out=pb[pkq][:, 128:192].bitcast(BF16),
                            in_=x_q[:, hb * 128:(hb + 1) * 128],
                            identity=identb[:]),
                       reads=[mxsB[2], B_("identb")], writes=[pbB[pkq]],
                       deps=[ec_xq])
                    xtoks.append(t)

                def xq_evac(hb, dep):
                    pkq = hb % 2
                    act(lambda e, tt=tt, hb=hb, pkq=pkq: e.activation(
                            out=xqT[:, tt * D + hb * 128: tt * D + (hb + 1) * 128],
                            in_=pb[pkq][:, 128:192].bitcast(BF16),
                            func=AF.Identity),
                        reads=[], writes=[pbB[pkq], xqTB[tt]], deps=[dep])
                xq_transpose(0)
                for hb in range(ND):
                    if hb + 1 < ND:
                        xq_transpose(hb + 1)
                        xq_evac(hb, xtoks[hb + 1])
                    else:
                        fl = pe_zfence(1 - (hb % 2))
                        xq_evac(hb, fl)
                xqT_echo[(st, tt)] = act_echo()
                dma("s", s_pk[st % 2][tt * 128:(tt + 1) * 128, :], lnt,
                    reads=[ln1B[tt]], writes=[B_(f"park{st % 2}_{tt}")])
                yield
            if tap == "xq":
                for tt in range(TT):
                    dma("p", d_y[b, t0 + tt * 128: t0 + (tt + 1) * 128, :],
                        x_q[:, 0:1024], reads=[mxsB[2]], sem="dout")

        def plan_ffn(st, tap=tap):
            b = st // (L // STG_T)
            t0 = (st % (L // STG_T)) * STG_T
            for ih in range(16):
                i0 = ih * 256
                wsl = ih % 2
                wgv = wg_sb[:, wsl * 2048:(wsl + 1) * 2048]
                wuv = wu_sb[:, wsl * 2048:(wsl + 1) * 2048]
                dma("s", wgv.rearrange("p (hb i) -> p hb i", hb=8),
                    s_wgT[:, i0:i0 + 256].rearrange("(hb p) i -> p hb i", p=128),
                    writes=[wgB[wsl]], sem="dwA")
                dma("s", wuv.rearrange("p (hb i) -> p hb i", hb=8),
                    s_wuT[:, i0:i0 + 256].rearrange("(hb p) i -> p hb i", p=128),
                    writes=[wuB[wsl]], sem="dwA")
                for tt in range(TT):
                    pg = 4 + (tt % 2)
                    pu = 6 + (tt % 2)

                    def mmg(e, tt=tt, pg=pg, wgv=wgv):
                        r = None
                        for hb in range(ND):
                            r = e.matmul(
                                out=pb[pg][:, 0:256],
                                lhsT=xqT[:, tt * D + hb * 128: tt * D + (hb + 1) * 128],
                                rhs=wgv[:, hb * 256:(hb + 1) * 256],
                                start=(hb == 0), stop=(hb == ND - 1))
                        return r
                    pe(mmg, reads=[xqTB[tt], wgB[wsl]], writes=[pbB[pg]],
                       deps=[xqT_echo.get((st, tt))] if xqT_echo.get((st, tt)) else [])

                    def mmu(e, tt=tt, pu=pu, wuv=wuv):
                        r = None
                        for hb in range(ND):
                            r = e.matmul(
                                out=pb[pu][:, 0:256],
                                lhsT=xqT[:, tt * D + hb * 128: tt * D + (hb + 1) * 128],
                                rhs=wuv[:, hb * 256:(hb + 1) * 256],
                                start=(hb == 0), stop=(hb == ND - 1))
                        return r
                    pe(mmu, reads=[xqTB[tt], wuB[wsl]], writes=[pbB[pu]])
                    f_g = pe_zfence(pg)
                    f_u = pe_zfence(pu)
                    gslice = gsb[:, 0:256]
                    sg0 = (4 if st % 2 == 0 else 16) + tt
                    su0 = (8 if st % 2 == 0 else 20) + tt
                    act(lambda e, pg=pg, gslice=gslice, sg0=sg0: e.activation(
                            out=gslice, in_=pb[pg][:, 0:256], func=AF.Silu,
                            scale=cols[:, sg0:sg0 + 1]),
                        reads=[colsB], writes=[pbB[pg], gsB[0]], deps=[f_g])
                    dve(lambda e, tt=tt, i0=i0, pu=pu, gslice=gslice, su0=su0:
                            e.scalar_tensor_tensor(
                            out=Pbuf[:, tt * INNER + i0: tt * INNER + i0 + 256],
                            in0=pb[pu][:, 0:256], scalar=cols[:, su0:su0 + 1],
                            in1=gslice, op0=ALU.mult, op1=ALU.mult),
                        reads=[colsB, gsB[0]], writes=[pbB[pu], PB[tt]], deps=[f_u])
                yield

            if tap == "P":
                for tt in range(TT):
                    dma("s", d_y[b, t0 + tt * 128: t0 + (tt + 1) * 128, :],
                        Pbuf[:, tt * INNER: tt * INNER + 1024],
                        reads=[PB[tt]], sem="dout")
                return

            for tt in range(TT):
                Pt = Pbuf[:, tt * INNER:(tt + 1) * INNER]
                qp = Pbuf.bitcast(BF16)[:, tt * 2 * INNER: tt * 2 * INNER + INNER]
                dnb = [4, 5] if tt % 2 == 0 else [6, 7]
                qb = 6 if tt % 2 == 0 else 4
                sb_ = 7 if tt % 2 == 0 else 5
                dve(lambda e, Pt=Pt: e.reduce_max(out=cols[:, 12:13], in_=Pt,
                                                  axis=AX.X,
                                                  apply_absolute_value=True),
                    reads=[PB[tt]], writes=[colsB])
                dve(lambda e: e.tensor_scalar(out=cols[:, 12:13],
                                              in0=cols[:, 12:13],
                                              scalar1=1e-5, scalar2=None,
                                              op0=ALU.max),
                    reads=[colsB], writes=[colsB], sync=True)
                dve(lambda e: e.reciprocal(out=cols[:, 13:14],
                                           in_=cols[:, 12:13]),
                    reads=[colsB], writes=[colsB], sync=True)
                dve(lambda e: e.tensor_scalar(out=cols[:, 13:14],
                                              in0=cols[:, 13:14],
                                              scalar1=127.0, scalar2=None,
                                              op0=ALU.mult),
                    reads=[colsB], writes=[colsB], sync=True)
                sd0 = (4 if st % 2 == 0 else 16) + tt
                dve(lambda e, sd0=sd0: e.tensor_tensor(out=cols[:, sd0:sd0 + 1],
                                                     in0=cols[:, 12:13],
                                                     in1=mw127[:, 2:3],
                                                     op=ALU.mult),
                    reads=[colsB, B_("mw127")], writes=[colsB], sync=True)
                if qpsum:
                    i32p = pb[qb][:, 0:512].bitcast(I32)
                    for q8 in range(8):
                        sl = Pt[:, q8 * 512:(q8 + 1) * 512]
                        act(lambda e, sl=sl, i32p=i32p: e.activation(
                                out=i32p, in_=sl, func=AF.Identity,
                                scale=cols[:, 13:14]),
                            reads=[PB[tt], colsB], writes=[pbB[qb]])
                        act(lambda e, qp=qp, q8=q8, i32p=i32p: e.activation(
                                out=qp[:, q8 * 512:(q8 + 1) * 512], in_=i32p,
                                func=AF.Identity),
                            reads=[], writes=[pbB[qb], PB[tt]])
                else:
                    for q8 in range(8):
                        sl = Pt[:, q8 * 512:(q8 + 1) * 512]
                        act(lambda e, sl=sl: e.activation(
                                out=i32s, in_=sl, func=AF.Identity,
                                scale=cols[:, 13:14]),
                            reads=[PB[tt], colsB], writes=[mxsB[0]])
                        act(lambda e, qp=qp, q8=q8: e.activation(
                                out=qp[:, q8 * 512:(q8 + 1) * 512], in_=i32s,
                                func=AF.Identity),
                            reads=[], writes=[mxsB[0], PB[tt]])
                trtoks = []

                def q_transpose(ib):
                    bnk = sb_ if ib % 2 == 0 else qb
                    t = pe(lambda e, qp=qp, ib=ib, bnk=bnk: e.transpose(
                            out=pb[bnk][:, 0:64].bitcast(BF16),
                            in_=qp[:, ib * 128:(ib + 1) * 128],
                            identity=identb[:]),
                       reads=[PB[tt], B_("identb")], writes=[pbB[bnk]])
                    trtoks.append(t)

                def q_evac(ib, dep):
                    bnk = sb_ if ib % 2 == 0 else qb
                    act(lambda e, ib=ib, bnk=bnk: e.activation(
                            out=qPT[:, ib * 128:(ib + 1) * 128],
                            in_=pb[bnk][:, 0:64].bitcast(BF16),
                            func=AF.Identity),
                        reads=[], writes=[pbB[bnk], qPTB[0]], deps=[dep])
                q_transpose(0)
                for ib in range(INNER // 128):
                    if ib + 1 < INNER // 128:
                        q_transpose(ib + 1)
                        q_evac(ib, trtoks[ib + 1])
                    else:
                        fl = pe_zfence(dnb[0])
                        q_evac(ib, fl)
                ec_qpt = act_echo()
                yield
                if use_park:
                    dma("s", out_t[:, 0:D], s_pk[st % 2][tt * 128:(tt + 1) * 128, :],
                        reads=[B_(f"park{st % 2}_{tt}")], writes=[outB[0]])
                for ibp in range(INNER // 256):
                    dma("s", wd_sb[:].rearrange("p (two d) -> p two d", two=2),
                        s_wdT[ibp * 256:(ibp + 1) * 256, :].rearrange(
                            "(two p) d -> p two d", p=128),
                        writes=[wdB[0], wdB[1]], sem="dwd")
                    for sub in range(2):
                        ib = ibp * 2 + sub
                        for dh in range(2):
                            pd = dnb[dh]
                            pe(lambda e, dh=dh, ib=ib, sub=sub, pd=pd: e.matmul(
                                    out=pb[pd][:, 0:512],
                                    lhsT=qPT[:, ib * 128:(ib + 1) * 128],
                                    rhs=wd_sb[:, sub * D + dh * 512:
                                              sub * D + dh * 512 + 512],
                                    start=(ib == 0), stop=(ib == INNER // 128 - 1)),
                               reads=[qPTB[0], wdB[sub]], writes=[pbB[pd]],
                               deps=[ec_qpt])
                fdn = [pe_zfence(dnb[0]), pe_zfence(dnb[1])]
                for dh in range(2):
                    pd = dnb[dh]
                    if use_park:
                        dve(lambda e, dh=dh, pd=pd, sd0=sd0: e.scalar_tensor_tensor(
                                out=out_t[:, dh * 512: dh * 512 + 512],
                                in0=pb[pd][:, 0:512], scalar=cols[:, sd0:sd0 + 1],
                                in1=out_t[:, dh * 512: dh * 512 + 512],
                                op0=ALU.mult, op1=ALU.add),
                            reads=[colsB], writes=[pbB[pd], outB[0]],
                            sync=True, deps=[fdn[dh]])
                    else:
                        dve(lambda e, dh=dh, pd=pd, sd0=sd0, tt=tt: e.scalar_tensor_tensor(
                                out=out_t[:, dh * 512: dh * 512 + 512],
                                in0=pb[pd][:, 0:512], scalar=cols[:, sd0:sd0 + 1],
                                in1=ln1[:, tt * D + dh * 512: tt * D + dh * 512 + 512],
                                op0=ALU.mult, op1=ALU.add),
                            reads=[colsB, ln1B[tt]], writes=[pbB[pd], outB[0]],
                            sync=True, deps=[fdn[dh]])
                ot = out_t[:, 0:D]
                dve(lambda e, ot=ot: e.bn_stats(out=bnst2[:, 0:6],
                                                in_=ot[:, 0:512]),
                    reads=[outB[0]], writes=[bn2B])
                dve(lambda e, ot=ot: e.bn_stats(out=bnst2[:, 6:12],
                                                in_=ot[:, 512:1024]),
                    reads=[outB[0]], writes=[bn2B])
                plan_mv_combine(14, mv2, bnst2, mv2B, bn2B)
                act(lambda e: e.activation(out=cols[:, 13:14], in_=mv2[:, 1:2],
                                           func=AF.Ln, bias=epsc[:]),
                    reads=[mv2B, B_("epsc")], writes=[colsB])
                act(lambda e: e.activation(out=cols[:, 13:14], in_=cols[:, 13:14],
                                           func=AF.Exp, scale=-0.5),
                    reads=[colsB], writes=[colsB], sync=True)
                dve(lambda e: e.scalar_tensor_tensor(out=cols[:, 14:15],
                                                     in0=mv2[:, 0:1],
                                                     scalar=-1.0,
                                                     in1=cols[:, 13:14],
                                                     op0=ALU.mult,
                                                     op1=ALU.mult),
                    reads=[mv2B, colsB], writes=[colsB], sync=True)
                act(lambda e, ot=ot: e.activation(out=ot, in_=ot,
                                                  func=AF.Identity,
                                                  scale=cols[:, 13:14],
                                                  bias=cols[:, 14:15]),
                    reads=[colsB, outB[0]], writes=[outB[0]], sync=True)
                dma("s", d_y[b, t0 + tt * 128: t0 + (tt + 1) * 128, :], ot,
                    reads=[outB[0]], sem="dout")
                yield

        nout = 0
        tracks = []
        for st in range(0 if tap == "wgq" else n_stages + 1):
            gens = []
            if do_mixer and st < n_stages:
                gens.append(plan_mixer(st, upto=mixer_upto))
            if do_ffn and st >= 1 and tap not in ("mixres", "ln1", "xq"):
                gens.append(plan_ffn(st - 1))
            if not overlap:
                for g in reversed(list(gens)):
                    for _ in g:
                        pass
                gens = []
            while gens:
                for gi, g in enumerate(list(gens)):
                    steps = 2 if (len(gens) > 1 and gi == 1) else 1
                    for _ in range(steps):
                        try:
                            next(g)
                        except StopIteration:
                            gens.remove(g)
                            break
        if P.cnt.get("dout", 0):
            P.raw_wait("gpsimd", [("dout", P.cnt["dout"])])
        else:
            P.raw_wait("gpsimd", [("dqs", 16)])

        globals()["_LAST_PLAN"] = P
        for eng in ["sync", "vector", "scalar", "tensor", "gpsimd"]:
            ops = [(w, f, i) for (e, w, f, i) in P.ops if e == eng]
            if not ops:
                continue

            def emit(e, ops=ops):
                for waits, fn, incs in ops:
                    for s, v in waits:
                        e.wait_ge(sems[s], v)
                    if fn is not None:
                        instr = fn(e)
                        instr.then_inc(sems[incs[0]], incs[1])
            getattr(block, eng)(emit)

    return nc


_NC_CACHE = {}


def kernel(L0_attn_x, L0_attn_W_x, L0_attn_W_dt, L0_attn_b_dt, L0_attn_A_log,
           L0_attn_D, L0_ffn_w_gate, L0_ffn_w_up, L0_ffn_w_down):
    x = np.ascontiguousarray(np.asarray(L0_attn_x, np.float32))
    wx = np.ascontiguousarray(np.asarray(L0_attn_W_x, np.float32))
    wdt = np.ascontiguousarray(np.asarray(L0_attn_W_dt, np.float32))
    bdt = np.asarray(L0_attn_b_dt, np.float32)
    alog = np.asarray(L0_attn_A_log, np.float32)
    dd = np.asarray(L0_attn_D, np.float32)
    wg = np.ascontiguousarray(np.asarray(L0_ffn_w_gate, np.float32))
    wu = np.ascontiguousarray(np.asarray(L0_ffn_w_up, np.float32))
    wd = np.ascontiguousarray(np.asarray(L0_ffn_w_down, np.float32))

    bdt_fm = np.ascontiguousarray(bdt.reshape(ND, 128).T)
    d_fm = np.ascontiguousarray(dd.reshape(ND, 128).T)
    alog_fm = np.ascontiguousarray(
        alog.reshape(ND, 128, NST).transpose(1, 0, 2).reshape(128, ND * NST))
    ident = np.eye(128, dtype=np.float32)

    if "nc" not in _NC_CACHE:
        _NC_CACHE["nc"] = build_program()
    nc = _NC_CACHE["nc"]

    in_maps = []
    for c in range(NCORES):
        in_maps.append({
            "x": x[c * B_CORE:(c + 1) * B_CORE],
            "w_x": wx, "w_dt": wdt, "b_dt_fm": bdt_fm, "a_log_fm": alog_fm,
            "d_fm": d_fm, "w_gate": wg, "w_up": wu, "w_down": wd, "ident": ident,
        })
    res = run_bass_kernel_spmd(nc, in_maps, core_ids=list(range(NCORES)))
    out = np.concatenate([np.asarray(r["y"]) for r in res.results], axis=0)
    return out.astype(np.float32)


# version 17 (final)
# speedup vs baseline: 1.1237x; 1.1237x over previous
import numpy as np
from contextlib import ExitStack

import concourse.bass as bass
import concourse.mybir as mybir
from concourse.bass_utils import run_bass_kernel_spmd

F32 = mybir.dt.float32
F32R = mybir.dt.float32r
BF16 = mybir.dt.bfloat16
FP16 = mybir.dt.float16
I32 = mybir.dt.int32
AF = mybir.ActivationFunctionType
ALU = mybir.AluOpType
AX = mybir.AxisListType

B_FULL, L, D = 16, 2048, 1024
NCORES = 8
B_CORE = B_FULL // NCORES
NST = 16
RK = 64
INNER = 4096
ND = 8
STG_T = 512
N_STAGES = (B_CORE * L) // STG_T
TT = STG_T // 128
EPS = 1e-5

ENG_SEM = {"tensor": "pe", "vector": "dve", "scalar": "act", "gpsimd": "pool"}


class Buf:
    __slots__ = ("name", "wtok", "rtoks")

    def __init__(self, name):
        self.name = name
        self.wtok = None
        self.rtoks = []


class Plan:
    def __init__(self):
        self.ops = []
        self.cnt = {}
        self.wm = {}

    def op(self, eng, fn, reads=(), writes=(), sem=None, k=1, deps=(), sync=False):
        if sem is None:
            sem = ENG_SEM[eng]
            k = 1
        waits = {}

        def need(tok):
            if tok is None:
                return
            s, v = tok
            if s == ENG_SEM.get(eng) and not sync:
                return
            if self.wm.get((eng, s), 0) >= v:
                return
            waits[s] = max(waits.get(s, 0), v)

        for b in reads:
            need(b.wtok)
        for b in writes:
            need(b.wtok)
            for t in b.rtoks:
                need(t)
        for t in deps:
            need(t)
        wlist = list(waits.items())
        for s, v in wlist:
            self.wm[(eng, s)] = v
        self.cnt[sem] = self.cnt.get(sem, 0) + k
        tok = (sem, self.cnt[sem])
        self.ops.append((eng, wlist, fn, (sem, k)))
        for b in reads:
            b.rtoks.append(tok)
        for b in writes:
            b.wtok = tok
            b.rtoks = []
        return tok

    def raw_wait(self, eng, toks):
        wlist = []
        for s, v in toks:
            if self.wm.get((eng, s), 0) < v:
                wlist.append((s, v))
                self.wm[(eng, s)] = v
        if wlist:
            self.ops.append((eng, wlist, None, None))


def build_program(n_stages=N_STAGES, do_prologue=True, do_mixer=True, do_ffn=True, mixer_upto=99, xp_dbs=ND, xp_tail=2, tap=None, fences=7, overlap=True, use_park=True, qpsum=True):
    nc = bass.Bass("TRN2", target_bir_lowering=False, debug=False)

    d_x = nc.dram_tensor("x", [B_CORE, L, D], F32, kind="ExternalInput").ap()
    d_wx = nc.dram_tensor("w_x", [D, 96], F32, kind="ExternalInput").ap()
    d_wdt = nc.dram_tensor("w_dt", [RK, D], F32, kind="ExternalInput").ap()
    d_bdt = nc.dram_tensor("b_dt_fm", [128, ND], F32, kind="ExternalInput").ap()
    d_alog = nc.dram_tensor("a_log_fm", [128, ND * NST], F32, kind="ExternalInput").ap()
    d_dfm = nc.dram_tensor("d_fm", [128, ND], F32, kind="ExternalInput").ap()
    d_wg = nc.dram_tensor("w_gate", [INNER, D], F32, kind="ExternalInput").ap()
    d_wu = nc.dram_tensor("w_up", [INNER, D], F32, kind="ExternalInput").ap()
    d_wd = nc.dram_tensor("w_down", [D, INNER], F32, kind="ExternalInput").ap()
    d_id = nc.dram_tensor("ident", [128, 128], F32, kind="ExternalInput").ap()
    d_y = nc.dram_tensor("y", [B_CORE, L, D], F32, kind="ExternalOutput").ap()
    s_wgT = nc.dram_tensor("scr_wgT", [D, INNER], BF16, kind="Internal").ap()
    s_wuT = nc.dram_tensor("scr_wuT", [D, INNER], BF16, kind="Internal").ap()
    s_wdT = nc.dram_tensor("scr_wdT", [INNER, D], BF16, kind="Internal").ap()
    s_bc = nc.dram_tensor("scr_bc", [2 * NST, STG_T], FP16, kind="Internal").ap()
    s_pk = [nc.dram_tensor(f"scr_park{i}", [STG_T, D], F32, kind="Internal").ap()
            for i in range(2)]

    P = Plan()

    with ExitStack() as stack:
        def sb(name, shape, dtype=F32):
            return stack.enter_context(nc.sbuf_tensor(name, shape, dtype))

        wx_r = sb("wx_r", [128, ND * 96], F32R)
        wdt_r = sb("wdt_r", [RK, D], F32R)
        a_neg = sb("a_neg", [128, ND * NST], F32)
        bdt_fm = sb("bdt_fm", [128, ND], F32)
        dp1_fm = sb("dp1_fm", [128, ND], F32)
        ident = sb("identsb", [128, 128], F32)
        identb = sb("identb", [128, 128], BF16)
        ones_col = sb("ones_col", [128, 1], F32)
        ones_row = sb("ones_row", [1, 128], F32)
        zero_col = sb("zero_col", [128, 1], F32)
        escr = sb("escr", [128, 2], F32)
        ascr = sb("ascr", [128, 2], F32)
        hstA = sb("hstA", [128, 128], F32)
        hstB = sb("hstB", [128, 128], F32)
        mw127 = sb("mw127", [128, 3], F32)
        swcol = sb("swcol", [128, 3], F32)
        x_stage = sb("x_stage", [128, TT * D], F32)
        x_tmp = sb("x_tmp", [128, 2 * STG_T], F32R)
        dtx_db = sb("dtx_db", [128, STG_T], F32)
        dt_db = sb("dt_db", [128, STG_T], F32)
        yac_db = sb("yac_db", [128, STG_T], F32)
        h_grp = sb("h_grp", [128, 2 * STG_T], F32)
        ln1 = sb("ln1", [128, TT * D], F32)
        mxs = sb("mxs", [128, 4 * STG_T], F32)
        brep = sb("brep", [128, NST * STG_T], FP16)
        crep = sb("crep", [128, NST * STG_T], FP16)
        xp_f = sb("xp_f", [96, STG_T], F32)
        dtr_r = sb("dtr_r", [128, STG_T], F32R)
        xqT = sb("xqT", [128, TT * D], BF16)
        Pbuf = sb("Pbuf", [128, TT * INNER], F32)
        qPT = sb("qPT", [128, INNER], BF16)
        gsb = sb("gsb", [128, 256], F32)
        wg_sb = sb("wg_sb", [128, 2 * 8 * 256], BF16)
        wu_sb = sb("wu_sb", [128, 2 * 8 * 256], BF16)
        wd_sb = sb("wd_sb", [128, 3 * D], BF16)
        out_t = sb("out_t", [128, D], F32)
        cols = sb("cols", [128, 24], F32)
        bnst = sb("bnst", [128, 12], F32)
        mv = sb("mv", [128, 4], F32)
        mv2 = sb("mv2", [128, 4], F32)
        bnst2 = sb("bnst2", [128, 12], F32)
        pool_scr = sb("pool_scr", [128, 4], F32)
        s1x1 = sb("s1x1", [1, 4], F32)
        epsc = sb("epsc", [128, 1], F32)

        pb = [stack.enter_context(nc.psum_tensor(f"pb{i}", [128, 512], F32))
              for i in range(8)]

        sems = {}
        for s in ["pe", "dve", "act", "pool", "dqs", "dqp", "dout", "dx", "dbr", "dwA", "dwd", "dsc", "dwt"]:
            sems[s] = stack.enter_context(nc.semaphore("sem_" + s))

        block = stack.enter_context(nc.Block())

        bufs = {}

        def B_(name):
            if name not in bufs:
                bufs[name] = Buf(name)
            return bufs[name]

        pbB = [B_(f"pb{i}") for i in range(8)]

        def dma(queue, out, in_, reads=(), writes=(), sem=None, deps=()):
            eng = {"s": "sync", "p": "gpsimd"}[queue]
            if sem is None:
                sem = {"s": "dqs", "p": "dqp"}[queue]
            return P.op(eng, lambda e, o=out, i=in_: e.dma_start(out=o, in_=i),
                        reads=reads, writes=writes, sem=sem, k=16, deps=deps)

        def act(fn, reads=(), writes=(), deps=(), sync=False):
            return P.op("scalar", fn, reads=reads, writes=writes, deps=deps, sync=sync)

        def dve(fn, reads=(), writes=(), deps=(), sync=False):
            return P.op("vector", fn, reads=reads, writes=writes, deps=deps, sync=sync)

        def pool(fn, reads=(), writes=(), deps=(), sync=False):
            return P.op("gpsimd", fn, reads=reads, writes=writes, deps=deps, sync=sync)

        def pe(fn, reads=(), writes=(), deps=(), sync=False):
            return P.op("tensor", fn, reads=reads, writes=writes, deps=deps, sync=sync)

        def dve_echo():
            return dve(lambda e: e.tensor_copy(out=escr[:], in_=zero_col[:].broadcast_to([128, 2])),
                       reads=[B_("zero_col")], writes=[B_("escr")])

        def act_echo():
            return act(lambda e: e.activation(out=ascr[:, 0:1], in_=zero_col[:],
                                              func=AF.Identity),
                       reads=[B_("zero_col")], writes=[B_("ascr")])

        colsB = B_("cols")
        bnB = B_("bnst")
        bn2B = B_("bnst2")
        mv2B = B_("mv2")
        mvB = B_("mv")

        dma("s", a_neg[:], d_alog, writes=[B_("a_neg")])
        dma("s", bdt_fm[:], d_bdt, writes=[B_("bdt_fm")])
        dma("s", dp1_fm[:], d_dfm, writes=[B_("dp1_fm")])
        dma("s", ident[:], d_id, writes=[B_("ident")])
        for db in range(ND):
            dma("s", x_stage[:, db * 96:(db + 1) * 96],
                d_wx[db * 128:(db + 1) * 128, :], writes=[B_("xs_pro")])
        dma("s", x_stage[0:RK, 1024:2048], d_wdt, writes=[B_("xs_pro")])
        _par_total = ("dqs", P.cnt.get("dqs", 0))
        for _nm in ["a_neg", "bdt_fm", "dp1_fm", "ident", "xs_pro"]:
            B_(_nm).wtok = _par_total
        dve(lambda e: e.tensor_copy(out=wx_r[:], in_=x_stage[:, 0:ND * 96]),
            reads=[B_("xs_pro")], writes=[B_("wx_r")])
        dve(lambda e: e.tensor_copy(out=wdt_r[:], in_=x_stage[0:RK, 1024:2048]),
            reads=[B_("xs_pro")], writes=[B_("wdt_r")])
        act(lambda e: e.activation(out=a_neg[:], in_=a_neg[:], func=AF.Exp),
            reads=[B_("a_neg")], writes=[B_("a_neg")])
        dve(lambda e: e.tensor_scalar(out=a_neg[:], in0=a_neg[:], scalar1=-1.0,
                                      scalar2=None, op0=ALU.mult),
            reads=[B_("a_neg")], writes=[B_("a_neg")])
        dve(lambda e: e.tensor_scalar(out=dp1_fm[:], in0=dp1_fm[:], scalar1=1.0,
                                      scalar2=None, op0=ALU.add),
            reads=[B_("dp1_fm")], writes=[B_("dp1_fm")])
        dve(lambda e: e.tensor_copy(out=identb[:], in_=ident[:]),
            reads=[B_("ident")], writes=[B_("identb")])
        dve(lambda e: e.memset(ones_col[:], 1.0), writes=[B_("ones_col")])
        dve(lambda e: e.memset(ones_row[:], 1.0), writes=[B_("ones_row")])
        dve(lambda e: e.memset(zero_col[:], 0.0), writes=[B_("zero_col")])
        dve(lambda e: e.memset(epsc[:], EPS), writes=[B_("epsc")])

        pbufB = B_("Pbuf_pro")
        wtile = [Pbuf[:, i * 512:(i + 1) * 512] for i in range(2)]
        wqi = Pbuf[:, 1024:1536].bitcast(I32)
        wqb = Pbuf[:, 1536:1792].bitcast(BF16)
        wacc = [Pbuf[:, 1792 + i:1793 + i] for i in range(2)]
        wqT_ev = [Pbuf[:, 2048 + i * 64:2048 + (i + 1) * 64].bitcast(BF16)
                  for i in range(2)]

        for (dw, scr, NR, NC_, wi) in ([(d_wg, s_wgT, INNER, D, 0),
                                       (d_wu, s_wuT, INNER, D, 1),
                                       (d_wd, s_wdT, D, INNER, 2)] if do_prologue else []):
            ntile = (NR // 128) * (NC_ // 512)
            ncb = NC_ // 512
            dve(lambda e: e.memset(Pbuf[:, 1792:1794], 0.0), writes=[pbufB])
            for j in range(ntile):
                r0, c0 = (j // ncb) * 128, (j % ncb) * 512
                tb = j % 2
                dma("s", wtile[tb], dw[r0:r0 + 128, c0:c0 + 512], writes=[pbufB], sem="dwt")
                dve(lambda e, tb=tb: e.tensor_reduce(
                        out=cols[:, 8 + tb:9 + tb], in_=wtile[tb], axis=AX.X,
                        op=ALU.add, apply_absolute_value=True),
                    reads=[pbufB], writes=[colsB])
                dve(lambda e, tb=tb: e.tensor_tensor(
                        out=wacc[tb], in0=wacc[tb], in1=cols[:, 8 + tb:9 + tb],
                        op=ALU.add),
                    reads=[colsB, pbufB], writes=[pbufB], sync=True)
            dve(lambda e: e.tensor_tensor(out=wacc[0], in0=wacc[0], in1=wacc[1],
                                          op=ALU.add),
                reads=[pbufB], writes=[pbufB], sync=True)
            pe(lambda e: e.matmul(out=pb[0][0:1, 0:1], lhsT=wacc[0], rhs=ones_col[:],
                                  start=True, stop=True),
               reads=[pbufB, B_("ones_col")], writes=[pbB[0]])
            act(lambda e, NR=NR, NC_=NC_, wi=wi: e.activation(
                    out=s1x1[:, wi:wi + 1], in_=pb[0][0:1, 0:1],
                    func=AF.Identity, scale=1.0 / float(NR * NC_)),
                reads=[], writes=[pbB[0], B_("s1x1")])
            dve(lambda e, wi=wi: e.tensor_scalar(
                    out=s1x1[:, wi:wi + 1], in0=s1x1[:, wi:wi + 1],
                    scalar1=1e-5, scalar2=None, op0=ALU.max),
                reads=[B_("s1x1")], writes=[B_("s1x1")])
            pe(lambda e, wi=wi: e.matmul(out=pb[0][:, 0:1], lhsT=ones_row[:],
                                         rhs=s1x1[0:1, wi:wi + 1],
                                         start=True, stop=True),
               reads=[B_("s1x1"), B_("ones_row")], writes=[pbB[0]])
            act(lambda e, wi=wi: e.activation(out=mw127[:, wi:wi + 1],
                                              in_=pb[0][:, 0:1],
                                              func=AF.Identity, scale=1.0 / 127.0),
                reads=[], writes=[pbB[0], B_("mw127")])
            act(lambda e: e.activation(out=cols[:, 15:16], in_=pb[0][:, 0:1],
                                       func=AF.Identity),
                reads=[], writes=[pbB[0], colsB])
            dve(lambda e, wi=wi: e.reciprocal(out=swcol[:, wi:wi + 1],
                                              in_=cols[:, 15:16]),
                reads=[colsB], writes=[B_("swcol")])
            for j in range(ntile):
                r0, c0 = (j // ncb) * 128, (j % ncb) * 512
                tb = j % 2
                dma("s", wtile[tb], dw[r0:r0 + 128, c0:c0 + 512], writes=[pbufB], sem="dwt")
                act(lambda e, tb=tb, wi=wi: e.activation(
                        out=wqi, in_=wtile[tb], func=AF.Identity,
                        scale=swcol[:, wi:wi + 1]),
                    reads=[pbufB, B_("swcol")], writes=[pbufB])
                dve(lambda e: e.tensor_scalar(out=wqb, in0=wqi, scalar1=1.0,
                                              scalar2=-1.0, op0=ALU.min,
                                              op1=ALU.max),
                    reads=[pbufB], writes=[pbufB])
                ec_wq = dve_echo()
                for cb in range(4):
                    pk = 2 + (cb % 2)
                    pe(lambda e, cb=cb, pk=pk: e.transpose(
                            out=pb[pk][:, 0:64].bitcast(BF16),
                            in_=wqb[:, cb * 128:(cb + 1) * 128],
                            identity=identb[:]),
                       reads=[pbufB, B_("identb")], writes=[pbB[pk]], deps=[ec_wq])
                    act(lambda e, cb=cb, pk=pk: e.activation(
                            out=wqT_ev[cb % 2], in_=pb[pk][:, 0:64].bitcast(BF16),
                            func=AF.Identity),
                        reads=[], writes=[pbB[pk], pbufB])
                    dma("s", scr[c0 + cb * 128:c0 + (cb + 1) * 128, r0:r0 + 128],
                        wqT_ev[cb % 2], reads=[pbufB], sem="dsc")

        if tap == "wgq":
            for blkr in range(8):
                t_ld = dma("s", Pbuf[:, 0:512].bitcast(BF16),
                           s_wgT[blkr * 128:(blkr + 1) * 128, 0:1024],
                           writes=[B_("Pbuf_pro")])
                t_cv = P.op("vector",
                            lambda e: e.tensor_copy(out=Pbuf[:, 4096:5120],
                                                    in_=Pbuf[:, 0:512].bitcast(BF16)),
                            reads=[B_("Pbuf_pro")], writes=[B_("Pbuf_pro")], sync=True)
                dma("s", d_y[0, blkr * 128:(blkr + 1) * 128, :], Pbuf[:, 4096:5120],
                    reads=[B_("Pbuf_pro")], sem="dout")
            P.raw_wait("gpsimd", [("dout", 16 * 8)])
        xstgB = [B_(f"xstg{t}") for t in range(TT)]
        xtmpB = [B_("x_tmp0"), B_("x_tmp1")]
        dtxB = B_("dtx_db")
        dtB = B_("dt_db")
        yacB = B_("yac_db")
        hgB = B_("h_grp")
        ln1B = [B_(f"ln1_{t}") for t in range(TT)]
        mxsB = [B_(f"mxs{i}") for i in range(4)]
        brepB = B_("brep")
        crepB = B_("crep")
        xpB = B_("xp_f")
        dtrB = B_("dtr_r")
        hstB_ = [B_("hstA"), B_("hstB")]
        xqTB = [B_(f"xqT{t}") for t in range(TT)]
        PB = [B_(f"P{t}") for t in range(TT)]
        qPTB = [B_("qPT0")]
        gsB = [B_("gs0"), B_("gs0")]
        wgB = [B_("wg0"), B_("wg1")]
        wuB = [B_("wu0"), B_("wu1")]
        wdB = [B_(f"wd{i}") for i in range(3)]
        outB = [B_("out0")]
        bcB = B_("s_bc")

        for t in range(TT):
            if pbufB.wtok is not None:
                PB[t].rtoks.append(pbufB.wtok)
            PB[t].rtoks.extend(pbufB.rtoks)
        xspro = B_("xs_pro")
        for t in range(TT):
            if xspro.wtok is not None:
                xstgB[t].rtoks.append(xspro.wtok)
            xstgB[t].rtoks.extend(xspro.rtoks)

        mxs_dA = [mxs[:, 0:512], mxs[:, 512:1024]]
        mxs_dBx = [mxs[:, 1024:1536], mxs[:, 1536:2048]]
        i32s = mxs[:, 0:512].bitcast(I32)
        x_q = mxs[:, 1024:1536].bitcast(BF16)

        hst = [hstA, hstB]
        xqT_echo = {}
        qPT_echo = {}

        def pe_zfence(bank):
            return pe(lambda e, bank=bank: e.matmul(
                    out=pb[bank][0:1, 0:1], lhsT=zero_col[0:1, :],
                    rhs=ones_col[0:1, :], start=False, stop=True),
                reads=[B_("zero_col"), B_("ones_col")], writes=[pbB[bank]])

        def plan_mv_combine(scr, mvx, bnx, mvxB, bnxB):
            dve(lambda e: e.bn_aggr(out=mvx[:, 0:2], in_=bnx[:, 0:6]),
                reads=[bnxB], writes=[mvxB], sync=True)
            dve(lambda e: e.bn_aggr(out=mvx[:, 2:4], in_=bnx[:, 6:12]),
                reads=[bnxB], writes=[mvxB], sync=True)
            dve(lambda e: e.tensor_tensor(out=cols[:, scr:scr + 1], in0=mvx[:, 0:1],
                                          in1=mvx[:, 2:3], op=ALU.subtract),
                reads=[mvxB], writes=[colsB], sync=True)
            dve(lambda e: e.tensor_tensor(out=cols[:, scr:scr + 1], in0=cols[:, scr:scr + 1],
                                          in1=cols[:, scr:scr + 1], op=ALU.mult),
                reads=[colsB], writes=[colsB], sync=True)
            dve(lambda e: e.tensor_tensor(out=mvx[:, 0:1], in0=mvx[:, 0:1],
                                          in1=mvx[:, 2:3], op=ALU.add),
                reads=[mvxB], writes=[mvxB], sync=True)
            dve(lambda e: e.tensor_scalar(out=mvx[:, 0:1], in0=mvx[:, 0:1],
                                          scalar1=0.5, scalar2=None, op0=ALU.mult),
                reads=[mvxB], writes=[mvxB], sync=True)
            dve(lambda e: e.tensor_tensor(out=mvx[:, 1:2], in0=mvx[:, 1:2],
                                          in1=mvx[:, 3:4], op=ALU.add),
                reads=[mvxB], writes=[mvxB], sync=True)
            dve(lambda e: e.scalar_tensor_tensor(out=mvx[:, 1:2], in0=cols[:, scr:scr + 1],
                                                 scalar=0.5, in1=mvx[:, 1:2],
                                                 op0=ALU.mult, op1=ALU.add),
                reads=[colsB, mvxB], writes=[mvxB], sync=True)
            dve(lambda e: e.tensor_scalar(out=mvx[:, 1:2], in0=mvx[:, 1:2],
                                          scalar1=0.5, scalar2=None, op0=ALU.mult),
                reads=[mvxB], writes=[mvxB], sync=True)

        def plan_mixer(st, upto=99, tap=tap):
            b = st // (L // STG_T)
            t0 = (st % (L // STG_T)) * STG_T
            seq_start = (st % (L // STG_T)) == 0
            hc, hp = hst[st % 2], hst[1 - st % 2]
            hcB, hpB = hstB_[st % 2], hstB_[1 - st % 2]

            for tt in range(TT):
                dma("s", x_stage[:, tt * D:(tt + 1) * D],
                    d_x[b, t0 + tt * 128: t0 + (tt + 1) * 128, :],
                    writes=[xstgB[tt]], sem="dx")

            for db in range(ND):
                pk = db % 2
                for tt in range(TT):
                    pe(lambda e, db=db, tt=tt, pk=pk: e.transpose(
                            out=pb[pk][:, tt * 128:(tt + 1) * 128],
                            in_=x_stage[:, tt * D + db * 128: tt * D + (db + 1) * 128],
                            identity=ident[:]),
                       reads=[xstgB[tt], B_("ident")], writes=[pbB[pk]])
                xv = x_tmp[:, (db % 2) * STG_T:(db % 2 + 1) * STG_T]
                dve(lambda e, pk=pk, xv=xv: e.tensor_copy(out=xv, in_=pb[pk][:]),
                    reads=[], writes=[pbB[pk], xtmpB[db % 2]])
                ec = dve_echo()
                pe(lambda e, db=db, xv=xv: e.matmul(
                        out=pb[2][0:96, :], lhsT=wx_r[:, db * 96:(db + 1) * 96],
                        rhs=xv, start=(db == 0), stop=(db == ND - 1)),
                   reads=[xtmpB[db % 2], B_("wx_r")], writes=[pbB[2]], deps=[ec])
            pe(lambda e: e.matmul(out=pb[1][0:1, 0:1], lhsT=ones_col[0:1, :],
                                  rhs=ones_col[0:1, :], start=True, stop=True),
               reads=[B_("ones_col")], writes=[pbB[1], pbB[2]])
            act(lambda e: e.activation(out=xp_f[:], in_=pb[2][0:96, :],
                                       func=AF.Identity),
                reads=[], writes=[pbB[2], xpB])
            dve(lambda e: e.tensor_copy(out=dtr_r[:], in_=pb[2][:]),
                reads=[], writes=[pbB[2], dtrB])
            ec_dtr = dve_echo()
            yield

            dma("p", s_bc, xp_f[64:96, :], reads=[xpB], writes=[bcB])
            for n in range(NST):
                dma("s", brep[:, n * STG_T:(n + 1) * STG_T],
                    s_bc[n:n + 1, :].broadcast_to([128, STG_T]),
                    reads=[bcB], writes=[brepB], sem="dbr")
                dma("s", crep[:, n * STG_T:(n + 1) * STG_T],
                    s_bc[NST + n:NST + n + 1, :].broadcast_to([128, STG_T]),
                    reads=[bcB], writes=[crepB], sem="dbr")
            yield

            for db in range(ND):
                pk = db % 2
                for tt in range(TT):
                    pe(lambda e, db=db, tt=tt, pk=pk: e.transpose(
                            out=pb[pk][:, tt * 128:(tt + 1) * 128],
                            in_=x_stage[:, tt * D + db * 128: tt * D + (db + 1) * 128],
                            identity=ident[:]),
                       reads=[xstgB[tt], B_("ident")], writes=[pbB[pk]])
                xv = x_tmp[:, (db % 2) * STG_T:(db % 2 + 1) * STG_T]
                dve(lambda e, pk=pk, xv=xv: e.tensor_copy(out=xv, in_=pb[pk][:]),
                    reads=[], writes=[pbB[pk], xtmpB[db % 2]])
                pk2 = 2 + db % 2
                pe(lambda e, db=db, pk2=pk2: e.matmul(
                        out=pb[pk2][:], lhsT=wdt_r[:, db * 128:(db + 1) * 128],
                        rhs=dtr_r[0:RK, :], start=True, stop=True),
                   reads=[dtrB, B_("wdt_r")], writes=[pbB[pk2]], deps=[ec_dtr])
                pe(lambda e, pk=pk: e.matmul(out=pb[pk][0:1, 0:1], lhsT=ones_col[0:1, :],
                                             rhs=ones_col[0:1, :], start=True, stop=True),
                   reads=[B_("ones_col")], writes=[pbB[pk], pbB[pk2]])
                act(lambda e, db=db, pk2=pk2: e.activation(
                        out=dt_db[:], in_=pb[pk2][:], func=AF.Exp,
                        bias=bdt_fm[:, db:db + 1]),
                    reads=[B_("bdt_fm")], writes=[pbB[pk2], dtB])
                act(lambda e: e.activation(out=dt_db[:], in_=dt_db[:], func=AF.Ln,
                                           bias=1.0),
                    reads=[dtB], writes=[dtB])
                dve(lambda e, xv=xv: e.tensor_tensor(out=dtx_db[:], in0=dt_db[:],
                                               in1=xv.bitcast(F32),
                                               op=ALU.mult),
                     reads=[dtB, xtmpB[db % 2]], writes=[dtxB])

                def emit_dA(n):
                    ab = n % 2
                    col = db * NST + n
                    act(lambda e, ab=ab, col=col: e.activation(
                            out=mxs_dA[ab], in_=dt_db[:], func=AF.Exp,
                            scale=a_neg[:, col:col + 1]),
                        reads=[dtB, B_("a_neg")], writes=[mxsB[ab]])

                def emit_dBx(n):
                    ab = n % 2
                    return dve(lambda e, n=n, ab=ab: e.tensor_tensor(
                            out=mxs_dBx[ab], in0=dtx_db[:],
                            in1=brep[:, n * STG_T:(n + 1) * STG_T], op=ALU.mult),
                         reads=[dtxB, brepB], writes=[mxsB[2 + ab]])

                emit_dA(0)
                fence = emit_dBx(0)
                for n in range(NST):
                    ab = n % 2
                    if n + 1 < NST:
                        emit_dA(n + 1)
                        fence = emit_dBx(n + 1)
                    else:
                        fence = pool(lambda e: e.tensor_copy(out=pool_scr[:],
                                                             in_=dtx_db[:, 0:4]),
                                     reads=[dtxB], writes=[B_("pool_scr")])
                    hsl = h_grp[:, (n % 2) * 512:(n % 2 + 1) * 512]
                    col = db * NST + n
                    if seq_start:
                        dve(lambda e, hsl=hsl, ab=ab: e.tensor_tensor_scan(
                                out=hsl, data0=mxs_dA[ab], data1=mxs_dBx[ab],
                                initial=0.0, op0=ALU.mult, op1=ALU.add),
                            reads=[mxsB[ab], mxsB[2 + ab]], writes=[hgB],
                            deps=[fence])
                    else:
                        dve(lambda e, hsl=hsl, ab=ab, col=col, hp=hp:
                                e.tensor_tensor_scan(
                                out=hsl, data0=mxs_dA[ab], data1=mxs_dBx[ab],
                                initial=hp[:, col:col + 1],
                                op0=ALU.mult, op1=ALU.add),
                            reads=[mxsB[ab], mxsB[2 + ab], hpB], writes=[hgB],
                            deps=[fence])
                    dve(lambda e, hsl=hsl, ab=ab, n=n: e.tensor_tensor(
                            out=mxs_dA[ab], in0=hsl,
                            in1=crep[:, n * STG_T:(n + 1) * STG_T], op=ALU.mult),
                        reads=[hgB, crepB], writes=[mxsB[ab]])
                    if n == 0:
                        dve(lambda e, ab=ab: e.tensor_copy(out=yac_db[:],
                                                           in_=mxs_dA[ab]),
                            reads=[mxsB[ab]], writes=[yacB])
                    else:
                        dve(lambda e, ab=ab: e.tensor_tensor(
                                out=yac_db[:], in0=yac_db[:], in1=mxs_dA[ab],
                                op=ALU.add),
                            reads=[mxsB[ab], yacB], writes=[yacB])
                    if n % 2 == 1:
                        g = n // 2
                        dve(lambda e, g=g, db=db, hc=hc: e.tensor_copy(
                                out=hc[:, db * NST + g * 2: db * NST + (g + 1) * 2],
                                in_=h_grp[:, 511::512]),
                            reads=[hgB], writes=[hcB])
                dve(lambda e, db=db, xv=xv: e.scalar_tensor_tensor(
                        out=yac_db[:], in0=xv.bitcast(F32),
                        scalar=dp1_fm[:, db:db + 1], in1=yac_db[:],
                        op0=ALU.mult, op1=ALU.add),
                    reads=[xtmpB[db % 2], B_("dp1_fm"), yacB], writes=[yacB])
                ec_y = dve_echo()
                for tt in range(TT):
                    pe(lambda e, tt=tt, pk=pk: e.transpose(
                            out=pb[pk][:, tt * 128:(tt + 1) * 128],
                            in_=yac_db[:, tt * 128:(tt + 1) * 128],
                            identity=ident[:]),
                       reads=[yacB, B_("ident")], writes=[pbB[pk]], deps=[ec_y])
                for tt in range(TT):
                    dve(lambda e, tt=tt, db=db, pk=pk: e.tensor_copy(
                            out=ln1[:, tt * D + db * 128: tt * D + (db + 1) * 128],
                            in_=pb[pk][:, tt * 128:(tt + 1) * 128]),
                        reads=[], writes=[pbB[pk], ln1B[tt]])
                yield

            if tap == "mixres":
                for tt in range(TT):
                    dma("s", d_y[b, t0 + tt * 128: t0 + (tt + 1) * 128, :],
                        ln1[:, tt * D:(tt + 1) * D], reads=[ln1B[tt]], sem="dout")
                return
            if upto < 6:
                return

            for tt in range(TT):
                lnt = ln1[:, tt * D:(tt + 1) * D]
                dve(lambda e, lnt=lnt: e.bn_stats(out=bnst[:, 0:6],
                                                  in_=lnt[:, 0:512]),
                    reads=[ln1B[tt]], writes=[bnB])
                dve(lambda e, lnt=lnt: e.bn_stats(out=bnst[:, 6:12],
                                                  in_=lnt[:, 512:1024]),
                    reads=[ln1B[tt]], writes=[bnB])
                plan_mv_combine(15, mv, bnst, mvB, bnB)
                act(lambda e: e.activation(out=cols[:, 0:1], in_=mv[:, 1:2],
                                           func=AF.Ln, bias=epsc[:]),
                    reads=[mvB, B_("epsc")], writes=[colsB])
                act(lambda e: e.activation(out=cols[:, 0:1], in_=cols[:, 0:1],
                                           func=AF.Exp, scale=-0.5),
                    reads=[colsB], writes=[colsB], sync=True)
                dve(lambda e: e.scalar_tensor_tensor(out=cols[:, 1:2],
                                                     in0=mv[:, 0:1], scalar=-1.0,
                                                     in1=cols[:, 0:1],
                                                     op0=ALU.mult, op1=ALU.mult),
                    reads=[mvB, colsB], writes=[colsB], sync=True)
                act(lambda e, lnt=lnt: e.activation(out=lnt, in_=lnt,
                                                    func=AF.Identity,
                                                    scale=cols[:, 0:1],
                                                    bias=cols[:, 1:2]),
                    reads=[colsB, ln1B[tt]], writes=[ln1B[tt]], sync=True)
                if tap == "ln1":
                    dma("s", d_y[b, t0 + tt * 128: t0 + (tt + 1) * 128, :],
                        lnt, reads=[ln1B[tt]], sem="dout")
                dve(lambda e, lnt=lnt: e.reduce_max(out=cols[:, 2:3], in_=lnt,
                                                    axis=AX.X,
                                                    apply_absolute_value=True),
                    reads=[ln1B[tt]], writes=[colsB])
                dve(lambda e: e.tensor_scalar(out=cols[:, 2:3], in0=cols[:, 2:3],
                                              scalar1=1e-5, scalar2=None,
                                              op0=ALU.max),
                    reads=[colsB], writes=[colsB], sync=True)
                dve(lambda e: e.reciprocal(out=cols[:, 3:4], in_=cols[:, 2:3]),
                    reads=[colsB], writes=[colsB], sync=True)
                dve(lambda e: e.tensor_scalar(out=cols[:, 3:4], in0=cols[:, 3:4],
                                              scalar1=127.0, scalar2=None,
                                              op0=ALU.mult),
                    reads=[colsB], writes=[colsB], sync=True)
                sg0 = (4 if st % 2 == 0 else 16) + tt
                su0 = (8 if st % 2 == 0 else 20) + tt
                dve(lambda e, sg0=sg0: e.tensor_tensor(out=cols[:, sg0:sg0 + 1],
                                                     in0=cols[:, 2:3],
                                                     in1=mw127[:, 0:1],
                                                     op=ALU.mult),
                    reads=[colsB, B_("mw127")], writes=[colsB], sync=True)
                dve(lambda e, su0=su0: e.tensor_tensor(out=cols[:, su0:su0 + 1],
                                                     in0=cols[:, 2:3],
                                                     in1=mw127[:, 1:2],
                                                     op=ALU.mult),
                    reads=[colsB, B_("mw127")], writes=[colsB], sync=True)
                for h2 in range(2):
                    sl = lnt[:, h2 * 512:(h2 + 1) * 512]
                    act(lambda e, sl=sl: e.activation(out=i32s, in_=sl,
                                                      func=AF.Identity,
                                                      scale=cols[:, 3:4]),
                        reads=[ln1B[tt], colsB], writes=[mxsB[0]])
                    act(lambda e, h2=h2: e.activation(
                            out=x_q[:, h2 * 512:(h2 + 1) * 512], in_=i32s,
                            func=AF.Identity),
                        reads=[mxsB[0]], writes=[mxsB[2]])
                ec_xq = act_echo()
                xtoks = []

                def xq_transpose(hb):
                    pkq = hb % 2
                    t = pe(lambda e, hb=hb, pkq=pkq: e.transpose(
                            out=pb[pkq][:, 128:192].bitcast(BF16),
                            in_=x_q[:, hb * 128:(hb + 1) * 128],
                            identity=identb[:]),
                       reads=[mxsB[2], B_("identb")], writes=[pbB[pkq]],
                       deps=[ec_xq])
                    xtoks.append(t)

                def xq_evac(hb, dep):
                    pkq = hb % 2
                    act(lambda e, tt=tt, hb=hb, pkq=pkq: e.activation(
                            out=xqT[:, tt * D + hb * 128: tt * D + (hb + 1) * 128],
                            in_=pb[pkq][:, 128:192].bitcast(BF16),
                            func=AF.Identity),
                        reads=[], writes=[pbB[pkq], xqTB[tt]], deps=[dep])
                xq_transpose(0)
                for hb in range(ND):
                    if hb + 1 < ND:
                        xq_transpose(hb + 1)
                        xq_evac(hb, xtoks[hb + 1])
                    else:
                        fl = pe_zfence(1 - (hb % 2))
                        xq_evac(hb, fl)
                xqT_echo[(st, tt)] = act_echo()
                dma("s", s_pk[st % 2][tt * 128:(tt + 1) * 128, :], lnt,
                    reads=[ln1B[tt]], writes=[B_(f"park{st % 2}_{tt}")])
                yield
            if tap == "xq":
                for tt in range(TT):
                    dma("p", d_y[b, t0 + tt * 128: t0 + (tt + 1) * 128, :],
                        x_q[:, 0:1024], reads=[mxsB[2]], sem="dout")

        def plan_ffn(st, tap=tap):
            b = st // (L // STG_T)
            t0 = (st % (L // STG_T)) * STG_T
            for ih in range(16):
                i0 = ih * 256
                wsl = ih % 2
                wgv = wg_sb[:, wsl * 2048:(wsl + 1) * 2048]
                wuv = wu_sb[:, wsl * 2048:(wsl + 1) * 2048]
                dma("s", wgv.rearrange("p (hb i) -> p hb i", hb=8),
                    s_wgT[:, i0:i0 + 256].rearrange("(hb p) i -> p hb i", p=128),
                    writes=[wgB[wsl]], sem="dwA")
                dma("s", wuv.rearrange("p (hb i) -> p hb i", hb=8),
                    s_wuT[:, i0:i0 + 256].rearrange("(hb p) i -> p hb i", p=128),
                    writes=[wuB[wsl]], sem="dwA")
                for tt in range(TT):
                    pg = 4 + (tt % 2)
                    pu = 6 + (tt % 2)

                    def mmg(e, tt=tt, pg=pg, wgv=wgv):
                        r = None
                        for hb in range(ND):
                            r = e.matmul(
                                out=pb[pg][:, 0:256],
                                lhsT=xqT[:, tt * D + hb * 128: tt * D + (hb + 1) * 128],
                                rhs=wgv[:, hb * 256:(hb + 1) * 256],
                                start=(hb == 0), stop=(hb == ND - 1))
                        return r
                    pe(mmg, reads=[xqTB[tt], wgB[wsl]], writes=[pbB[pg]],
                       deps=[xqT_echo.get((st, tt))] if xqT_echo.get((st, tt)) else [])

                    def mmu(e, tt=tt, pu=pu, wuv=wuv):
                        r = None
                        for hb in range(ND):
                            r = e.matmul(
                                out=pb[pu][:, 0:256],
                                lhsT=xqT[:, tt * D + hb * 128: tt * D + (hb + 1) * 128],
                                rhs=wuv[:, hb * 256:(hb + 1) * 256],
                                start=(hb == 0), stop=(hb == ND - 1))
                        return r
                    pe(mmu, reads=[xqTB[tt], wuB[wsl]], writes=[pbB[pu]])
                    f_g = pe_zfence(pg)
                    f_u = pe_zfence(pu)
                    gslice = gsb[:, 0:256]
                    sg0 = (4 if st % 2 == 0 else 16) + tt
                    su0 = (8 if st % 2 == 0 else 20) + tt
                    act(lambda e, pg=pg, gslice=gslice, sg0=sg0: e.activation(
                            out=gslice, in_=pb[pg][:, 0:256], func=AF.Silu,
                            scale=cols[:, sg0:sg0 + 1]),
                        reads=[colsB], writes=[pbB[pg], gsB[0]], deps=[f_g])
                    dve(lambda e, tt=tt, i0=i0, pu=pu, gslice=gslice, su0=su0:
                            e.scalar_tensor_tensor(
                            out=Pbuf[:, tt * INNER + i0: tt * INNER + i0 + 256],
                            in0=pb[pu][:, 0:256], scalar=cols[:, su0:su0 + 1],
                            in1=gslice, op0=ALU.mult, op1=ALU.mult),
                        reads=[colsB, gsB[0]], writes=[pbB[pu], PB[tt]], deps=[f_u])
                yield

            if tap == "P":
                for tt in range(TT):
                    dma("s", d_y[b, t0 + tt * 128: t0 + (tt + 1) * 128, :],
                        Pbuf[:, tt * INNER: tt * INNER + 1024],
                        reads=[PB[tt]], sem="dout")
                return

            for tt in range(TT):
                Pt = Pbuf[:, tt * INNER:(tt + 1) * INNER]
                qp = Pbuf.bitcast(BF16)[:, tt * 2 * INNER: tt * 2 * INNER + INNER]
                dnb = [4, 5] if tt % 2 == 0 else [6, 7]
                qb = 6 if tt % 2 == 0 else 4
                sb_ = 7 if tt % 2 == 0 else 5
                dve(lambda e, Pt=Pt: e.reduce_max(out=cols[:, 12:13], in_=Pt,
                                                  axis=AX.X,
                                                  apply_absolute_value=True),
                    reads=[PB[tt]], writes=[colsB])
                dve(lambda e: e.tensor_scalar(out=cols[:, 12:13],
                                              in0=cols[:, 12:13],
                                              scalar1=1e-5, scalar2=None,
                                              op0=ALU.max),
                    reads=[colsB], writes=[colsB], sync=True)
                dve(lambda e: e.reciprocal(out=cols[:, 13:14],
                                           in_=cols[:, 12:13]),
                    reads=[colsB], writes=[colsB], sync=True)
                dve(lambda e: e.tensor_scalar(out=cols[:, 13:14],
                                              in0=cols[:, 13:14],
                                              scalar1=127.0, scalar2=None,
                                              op0=ALU.mult),
                    reads=[colsB], writes=[colsB], sync=True)
                sd0 = (4 if st % 2 == 0 else 16) + tt
                dve(lambda e, sd0=sd0: e.tensor_tensor(out=cols[:, sd0:sd0 + 1],
                                                     in0=cols[:, 12:13],
                                                     in1=mw127[:, 2:3],
                                                     op=ALU.mult),
                    reads=[colsB, B_("mw127")], writes=[colsB], sync=True)
                if qpsum:
                    i32p = pb[qb][:, 0:512].bitcast(I32)
                    for q8 in range(8):
                        sl = Pt[:, q8 * 512:(q8 + 1) * 512]
                        act(lambda e, sl=sl, i32p=i32p: e.activation(
                                out=i32p, in_=sl, func=AF.Identity,
                                scale=cols[:, 13:14]),
                            reads=[PB[tt], colsB], writes=[pbB[qb]])
                        act(lambda e, qp=qp, q8=q8, i32p=i32p: e.activation(
                                out=qp[:, q8 * 512:(q8 + 1) * 512], in_=i32p,
                                func=AF.Identity),
                            reads=[], writes=[pbB[qb], PB[tt]])
                else:
                    for q8 in range(8):
                        sl = Pt[:, q8 * 512:(q8 + 1) * 512]
                        act(lambda e, sl=sl: e.activation(
                                out=i32s, in_=sl, func=AF.Identity,
                                scale=cols[:, 13:14]),
                            reads=[PB[tt], colsB], writes=[mxsB[0]])
                        act(lambda e, qp=qp, q8=q8: e.activation(
                                out=qp[:, q8 * 512:(q8 + 1) * 512], in_=i32s,
                                func=AF.Identity),
                            reads=[], writes=[mxsB[0], PB[tt]])
                trtoks = []

                def q_transpose(ib):
                    bnk = sb_ if ib % 2 == 0 else qb
                    t = pe(lambda e, qp=qp, ib=ib, bnk=bnk: e.transpose(
                            out=pb[bnk][:, 0:64].bitcast(BF16),
                            in_=qp[:, ib * 128:(ib + 1) * 128],
                            identity=identb[:]),
                       reads=[PB[tt], B_("identb")], writes=[pbB[bnk]])
                    trtoks.append(t)

                def q_evac(ib, dep):
                    bnk = sb_ if ib % 2 == 0 else qb
                    act(lambda e, ib=ib, bnk=bnk: e.activation(
                            out=qPT[:, ib * 128:(ib + 1) * 128],
                            in_=pb[bnk][:, 0:64].bitcast(BF16),
                            func=AF.Identity),
                        reads=[], writes=[pbB[bnk], qPTB[0]], deps=[dep])
                q_transpose(0)
                for ib in range(INNER // 128):
                    if ib + 1 < INNER // 128:
                        q_transpose(ib + 1)
                        q_evac(ib, trtoks[ib + 1])
                    else:
                        fl = pe_zfence(dnb[0])
                        q_evac(ib, fl)
                ec_qpt = act_echo()
                yield
                if use_park:
                    dma("s", out_t[:, 0:D], s_pk[st % 2][tt * 128:(tt + 1) * 128, :],
                        reads=[B_(f"park{st % 2}_{tt}")], writes=[outB[0]])
                for ib in range(INNER // 128):
                    wb = ib % 3
                    dma("s", wd_sb[:, wb * D:(wb + 1) * D],
                        s_wdT[ib * 128:(ib + 1) * 128, :], writes=[wdB[wb]],
                        sem="dwd")
                    for dh in range(2):
                        pd = dnb[dh]
                        pe(lambda e, dh=dh, ib=ib, wb=wb, pd=pd: e.matmul(
                                out=pb[pd][:, 0:512],
                                lhsT=qPT[:, ib * 128:(ib + 1) * 128],
                                rhs=wd_sb[:, wb * D + dh * 512:
                                          wb * D + dh * 512 + 512],
                                start=(ib == 0), stop=(ib == INNER // 128 - 1)),
                           reads=[qPTB[0], wdB[wb]], writes=[pbB[pd]],
                           deps=[ec_qpt])
                fdn = [pe_zfence(dnb[0]), pe_zfence(dnb[1])]
                for dh in range(2):
                    pd = dnb[dh]
                    if use_park:
                        dve(lambda e, dh=dh, pd=pd, sd0=sd0: e.scalar_tensor_tensor(
                                out=out_t[:, dh * 512: dh * 512 + 512],
                                in0=pb[pd][:, 0:512], scalar=cols[:, sd0:sd0 + 1],
                                in1=out_t[:, dh * 512: dh * 512 + 512],
                                op0=ALU.mult, op1=ALU.add),
                            reads=[colsB], writes=[pbB[pd], outB[0]],
                            sync=True, deps=[fdn[dh]])
                    else:
                        dve(lambda e, dh=dh, pd=pd, sd0=sd0, tt=tt: e.scalar_tensor_tensor(
                                out=out_t[:, dh * 512: dh * 512 + 512],
                                in0=pb[pd][:, 0:512], scalar=cols[:, sd0:sd0 + 1],
                                in1=ln1[:, tt * D + dh * 512: tt * D + dh * 512 + 512],
                                op0=ALU.mult, op1=ALU.add),
                            reads=[colsB, ln1B[tt]], writes=[pbB[pd], outB[0]],
                            sync=True, deps=[fdn[dh]])
                ot = out_t[:, 0:D]
                dve(lambda e, ot=ot: e.bn_stats(out=bnst2[:, 0:6],
                                                in_=ot[:, 0:512]),
                    reads=[outB[0]], writes=[bn2B])
                dve(lambda e, ot=ot: e.bn_stats(out=bnst2[:, 6:12],
                                                in_=ot[:, 512:1024]),
                    reads=[outB[0]], writes=[bn2B])
                plan_mv_combine(14, mv2, bnst2, mv2B, bn2B)
                act(lambda e: e.activation(out=cols[:, 13:14], in_=mv2[:, 1:2],
                                           func=AF.Ln, bias=epsc[:]),
                    reads=[mv2B, B_("epsc")], writes=[colsB])
                act(lambda e: e.activation(out=cols[:, 13:14], in_=cols[:, 13:14],
                                           func=AF.Exp, scale=-0.5),
                    reads=[colsB], writes=[colsB], sync=True)
                dve(lambda e: e.scalar_tensor_tensor(out=cols[:, 14:15],
                                                     in0=mv2[:, 0:1],
                                                     scalar=-1.0,
                                                     in1=cols[:, 13:14],
                                                     op0=ALU.mult,
                                                     op1=ALU.mult),
                    reads=[mv2B, colsB], writes=[colsB], sync=True)
                act(lambda e, ot=ot: e.activation(out=ot, in_=ot,
                                                  func=AF.Identity,
                                                  scale=cols[:, 13:14],
                                                  bias=cols[:, 14:15]),
                    reads=[colsB, outB[0]], writes=[outB[0]], sync=True)
                dma("s", d_y[b, t0 + tt * 128: t0 + (tt + 1) * 128, :], ot,
                    reads=[outB[0]], sem="dout")
                yield

        nout = 0
        tracks = []
        for st in range(0 if tap == "wgq" else n_stages + 1):
            gens = []
            if do_mixer and st < n_stages:
                gens.append(plan_mixer(st, upto=mixer_upto))
            if do_ffn and st >= 1 and tap not in ("mixres", "ln1", "xq"):
                gens.append(plan_ffn(st - 1))
            if not overlap:
                for g in reversed(list(gens)):
                    for _ in g:
                        pass
                gens = []
            while gens:
                for gi, g in enumerate(list(gens)):
                    steps = 2 if (len(gens) > 1 and gi == 1) else 1
                    for _ in range(steps):
                        try:
                            next(g)
                        except StopIteration:
                            gens.remove(g)
                            break
        if P.cnt.get("dout", 0):
            P.raw_wait("gpsimd", [("dout", P.cnt["dout"])])
        else:
            P.raw_wait("gpsimd", [("dqs", 16)])

        globals()["_LAST_PLAN"] = P
        for eng in ["sync", "vector", "scalar", "tensor", "gpsimd"]:
            ops = [(w, f, i) for (e, w, f, i) in P.ops if e == eng]
            if not ops:
                continue

            def emit(e, ops=ops):
                for waits, fn, incs in ops:
                    for s, v in waits:
                        e.wait_ge(sems[s], v)
                    if fn is not None:
                        instr = fn(e)
                        instr.then_inc(sems[incs[0]], incs[1])
            getattr(block, eng)(emit)

    return nc


_NC_CACHE = {}


def kernel(L0_attn_x, L0_attn_W_x, L0_attn_W_dt, L0_attn_b_dt, L0_attn_A_log,
           L0_attn_D, L0_ffn_w_gate, L0_ffn_w_up, L0_ffn_w_down):
    x = np.ascontiguousarray(np.asarray(L0_attn_x, np.float32))
    wx = np.ascontiguousarray(np.asarray(L0_attn_W_x, np.float32))
    wdt = np.ascontiguousarray(np.asarray(L0_attn_W_dt, np.float32))
    bdt = np.asarray(L0_attn_b_dt, np.float32)
    alog = np.asarray(L0_attn_A_log, np.float32)
    dd = np.asarray(L0_attn_D, np.float32)
    wg = np.ascontiguousarray(np.asarray(L0_ffn_w_gate, np.float32))
    wu = np.ascontiguousarray(np.asarray(L0_ffn_w_up, np.float32))
    wd = np.ascontiguousarray(np.asarray(L0_ffn_w_down, np.float32))

    bdt_fm = np.ascontiguousarray(bdt.reshape(ND, 128).T)
    d_fm = np.ascontiguousarray(dd.reshape(ND, 128).T)
    alog_fm = np.ascontiguousarray(
        alog.reshape(ND, 128, NST).transpose(1, 0, 2).reshape(128, ND * NST))
    ident = np.eye(128, dtype=np.float32)

    if "nc" not in _NC_CACHE:
        _NC_CACHE["nc"] = build_program()
    nc = _NC_CACHE["nc"]

    in_maps = []
    for c in range(NCORES):
        in_maps.append({
            "x": x[c * B_CORE:(c + 1) * B_CORE],
            "w_x": wx, "w_dt": wdt, "b_dt_fm": bdt_fm, "a_log_fm": alog_fm,
            "d_fm": d_fm, "w_gate": wg, "w_up": wu, "w_down": wd, "ident": ident,
        })
    res = run_bass_kernel_spmd(nc, in_maps, core_ids=list(range(NCORES)))
    out = np.concatenate([np.asarray(r["y"]) for r in res.results], axis=0)
    return out.astype(np.float32)
